# version 07
# baseline (speedup 1.0000x reference)
import numpy as np
from contextlib import ExitStack

import concourse.bass as bass
import concourse.mybir as mybir
from concourse.bass_utils import run_bass_kernel_spmd

F32 = mybir.dt.float32
BF16 = mybir.dt.bfloat16
AF = mybir.ActivationFunctionType
ALU = mybir.AluOpType

B, T, C, H, DH = 4, 8192, 2048, 16, 128
TOK = 4096
NCH = 16
T1, T2, T3 = 128, 512, 256
N1, N2, N3 = TOK // T1, TOK // T2, TOK // T3
KEXP, R = 8, 64
RTOT = KEXP * R


def build_program():
    nc = bass.Bass("TRN2", target_bir_lowering=False)

    xT = nc.dram_tensor("xT", [C, TOK], F32, kind="ExternalInput")
    wqT = nc.dram_tensor("wqT", [C, C], F32, kind="ExternalInput")
    wkT = nc.dram_tensor("wkT", [C, C], F32, kind="ExternalInput")
    wvT = nc.dram_tensor("wvT", [C, C], F32, kind="ExternalInput")
    woT = nc.dram_tensor("woT", [C, C], F32, kind="ExternalInput")
    w1 = nc.dram_tensor("w1", [C, C], F32, kind="ExternalInput")
    w2c = nc.dram_tensor("w2c", [128, NCH, KEXP], F32, kind="ExternalInput")
    b1c_d = nc.dram_tensor("b1c", [128, NCH], F32, kind="ExternalInput")
    b2c_d = nc.dram_tensor("b2c", [KEXP, 1], F32, kind="ExternalInput")
    vallT = nc.dram_tensor("vallT", [C, RTOT], F32, kind="ExternalInput")
    uallT = nc.dram_tensor("uallT", [RTOT, C], F32, kind="ExternalInput")
    outT = nc.dram_tensor("outT", [C, TOK], F32, kind="ExternalOutput")
    y1raw_d = nc.dram_tensor("y1raw_d", [C, TOK], F32)
    mstd_d = nc.dram_tensor("mstd_d", [N2, 2 * T2], F32)
    ccin = nc.dram_tensor("ccin", [128, 2064], F32)
    ccout = nc.dram_tensor("ccout", [128, 2064], F32)

    st = ExitStack()
    sem = {}
    for name in [
        "ic1", "gI",
        "wka", "wkb", "wva", "wvb", "x0", "x1",
        "kg1", "vg1", "sc1", "ev1", "kvf", "pe1", "kse", "ccs",
        "ccd", "ccr", "dl2", "wqa", "wqb", "woa", "wob",
        "xb2", "qd2", "sc2", "qe2", "dend2", "zd2", "num2", "numc2",
        "zbm2", "mx2", "attn2", "xfa", "xfb", "y12", "y1sa", "y1sb",
        "cst2", "stt2", "msv2", "rs2", "msd2",
        "y3a", "y3b", "mra", "mrb", "hl3", "bc3", "nr3", "yc3",
        "hg3", "hr3", "lg3", "sm3", "se3", "ds3", "rc3", "rb3",
        "al3", "eg3", "vx3", "vc3", "vs3", "ug3", "st3", "ue3",
        "c23", "mv3", "r23", "b23", "n23", "o3a", "o3b", "o3c", "o3d",
        "w1a", "w1b", "wvu", "vc1", "zbc2", "ec3", "rcb3",
    ]:
        sem[name] = st.enter_context(nc.semaphore(name))
    S = type("S", (), sem)

    ones_col = st.enter_context(nc.sbuf_tensor("ones_col", [128, 1], BF16))
    ones_row = st.enter_context(nc.sbuf_tensor("ones_row", [1, 128], BF16))
    ones8 = st.enter_context(nc.sbuf_tensor("ones8", [8, 1], BF16))
    sel_bf = st.enter_context(nc.sbuf_tensor("sel_bf", [128, 2048], BF16))
    m1sel = st.enter_context(nc.sbuf_tensor("m1sel", [128, 512], BF16))
    b1c = st.enter_context(nc.sbuf_tensor("b1c_s", [128, NCH], F32))
    b2c = st.enter_context(nc.sbuf_tensor("b2c_s", [8, 1], F32))
    w2s = st.enter_context(nc.sbuf_tensor("w2s", [128, NCH, KEXP], BF16))
    dl_bf = st.enter_context(nc.sbuf_tensor("dl_bf", [128, 256], BF16))
    kv_bf = st.enter_context(nc.sbuf_tensor("kv_bf", [128, 2048], BF16))
    ksumf = st.enter_context(nc.sbuf_tensor("ksumf", [128, 16], F32))
    cpool = st.enter_context(nc.sbuf_tensor("cpool", [128, 2], F32))
    nc.const_aps.register(F32, 1e-5, cpool[:, 0:1])
    nc.const_aps.register(F32, -0.5, cpool[:, 1:2])

    if True:
        sti = ExitStack()
        selv1 = sti.enter_context(nc.sbuf_tensor("selv1", [128, 2048], F32))
        selv2 = sti.enter_context(nc.sbuf_tensor("selv2", [128, 2048], F32))
        m1v1 = sti.enter_context(nc.sbuf_tensor("m1v1", [128, 512], F32))
        m1v2 = sti.enter_context(nc.sbuf_tensor("m1v2", [128, 512], F32))
        with nc.Block("init") as blk:
            @blk.gpsimd
            def _(g):
                g.iota(selv1[:, :], [[1, 16], [0, 128]], channel_multiplier=0,
                       allow_small_or_imprecise_dtypes=True)
                g.iota(selv2[:, :], [[0, 2048]], channel_multiplier=1,
                       allow_small_or_imprecise_dtypes=True)
                g.iota(m1v1[:, :], [[1, 8], [0, 64]], channel_multiplier=0,
                       allow_small_or_imprecise_dtypes=True)
                g.iota(m1v2[:, :], [[0, 512]], channel_multiplier=1,
                       allow_small_or_imprecise_dtypes=True)
                g.memset(ones_col[:, :], 1.0)
                g.memset(ones_row[:, :], 1.0)
                g.memset(ones8[:, :], 1.0)
                g.memset(dl_bf[:, :], 0.0)
                g.memset(cpool[:, 0:1], 1e-5)
                g.memset(cpool[:, 1:2], -0.5).then_inc(S.gI, 1)
                g.dma_start(out=b1c[:, :], in_=b1c_d.ap()).then_inc(S.ic1, 16)
                g.dma_start(out=b2c[:, :], in_=b2c_d.ap()).then_inc(S.ic1, 16)
                g.dma_start(out=w2s[:, :, :], in_=w2c.ap()).then_inc(S.ic1, 16)

            @blk.vector
            def _(v):
                v.wait_ge(S.gI, 1)
                v.tensor_tensor(sel_bf[:, :], selv1[:, :], selv2[:, :], ALU.is_equal)
                v.tensor_tensor(m1sel[:, :], m1v1[:, :], m1v2[:, :], ALU.is_equal)

        sti.close()

        with ExitStack() as st1:
            wk_sb = st1.enter_context(nc.sbuf_tensor("wk_sb", [128, NCH, 2048], BF16))
            wv_sb = st1.enter_context(nc.sbuf_tensor("wv_sb", [128, NCH, 2048], BF16))
            xs_sb = st1.enter_context(nc.sbuf_tensor("xs_sb", [128, 2, NCH, T1], BF16))
            k_sb = st1.enter_context(nc.sbuf_tensor("k_sb", [128, 2048], BF16))
            v_sb = st1.enter_context(nc.sbuf_tensor("v_sb", [128, 2048], BF16))
            e1_sb = st1.enter_context(nc.sbuf_tensor("e1_sb", [128, 512], F32))
            r1_sb = st1.enter_context(nc.sbuf_tensor("r1_sb", [128, 512], F32))
            kvpack = st1.enter_context(nc.sbuf_tensor("kvpack", [128, 2064], F32))
            k_ps = st1.enter_context(nc.psum_tensor("k_ps", [128, 512], F32))
            v_ps = st1.enter_context(nc.psum_tensor("v_ps", [128, 512], F32))
            kv_ps = st1.enter_context(nc.psum_tensor("kv_ps", [128, 2048], F32))
            ksum_ps = st1.enter_context(nc.psum_tensor("ksum_ps", [128, 16], F32))

            xTr = xT.ap().rearrange("(cc p) t -> p cc t", p=128)
            wkr = wkT.ap().rearrange("(cc p) o -> p cc o", p=128)
            wvr = wvT.ap().rearrange("(cc p) o -> p cc o", p=128)

            with nc.Block("sweep1") as blk:
                @blk.gpsimd
                def _(g):
                    xsem = [S.x0, S.x1]
                    g.dma_start(out=xs_sb[:, 0, :, :], in_=xTr[:, :, 0:T1]).then_inc(S.x0, 16)
                    g.dma_start(out=xs_sb[:, 1, :, :], in_=xTr[:, :, T1:2 * T1]).then_inc(S.x1, 16)
                    wsem = {("k", 0): S.wka, ("k", 1): S.wkb, ("v", 0): S.wva, ("v", 1): S.wvb}
                    for ob in range(4):
                        for wname, wr, wsb in (("k", wkr, wk_sb), ("v", wvr, wv_sb)):
                            sm_ = wsem[(wname, ob % 2)]
                            g.dma_start(out=wsb[:, :, ob * 512:(ob + 1) * 512],
                                        in_=wr[:, :, ob * 512:(ob + 1) * 512]).then_inc(sm_, 16)
                    for t in range(2, N1):
                        g.wait_ge(S.pe1, t - 1)
                        g.dma_start(out=xs_sb[:, t % 2, :, :],
                                    in_=xTr[:, :, t * T1:(t + 1) * T1]).then_inc(xsem[t % 2], 16)
                    g.wait_ge(S.kvf, N1)
                    g.wait_ge(S.kse, 1)
                    g.dma_start(out=ccin.ap(), in_=kvpack[:, :]).then_inc(S.ccs, 16)
                    g.wait_ge(S.ccs, 16)

                @blk.tensor
                def _(tt):
                    xsem = [S.x0, S.x1]
                    wsem = {("k", 0): S.wka, ("k", 1): S.wkb, ("v", 0): S.wva, ("v", 1): S.wvb}
                    for t in range(N1):
                        tt.wait_ge(xsem[t % 2], 16 * (t // 2 + 1))
                        for ob in range(4):
                            gidx = 4 * t + ob
                            if t == 0:
                                tt.wait_ge(wsem[("k", ob % 2)], 16 * (ob // 2 + 1))
                            if gidx > 0:
                                tt.wait_ge(S.sc1, 2 * gidx)
                            for c in range(NCH):
                                ins = tt.matmul(k_ps[:, :], xs_sb[:, t % 2, c, :],
                                                wk_sb[:, c, ob * 512:(ob + 1) * 512],
                                                start=(c == 0), stop=(c == NCH - 1))
                                if c == NCH - 1:
                                    ins.then_inc(S.kg1, 1)
                            if t == 0:
                                tt.wait_ge(wsem[("v", ob % 2)], 16 * (ob // 2 + 1))
                            if gidx > 0:
                                tt.wait_ge(S.vc1, gidx)
                            for c in range(NCH):
                                ins = tt.matmul(v_ps[:, :], xs_sb[:, t % 2, c, :],
                                                wv_sb[:, c, ob * 512:(ob + 1) * 512],
                                                start=(c == 0), stop=(c == NCH - 1))
                                if c == NCH - 1:
                                    ins.then_inc(S.vg1, 1)
                        tt.wait_ge(S.ev1, 4 * t + 4)
                        tt.wait_ge(S.vc1, 4 * t + 4)
                        if t > 0:
                            tt.wait_ge(S.kvf, t)
                        for h in range(H):
                            tt.matmul(kv_ps[:, h * 128:(h + 1) * 128],
                                      k_sb[:, h * 128:(h + 1) * 128],
                                      v_sb[:, h * 128:(h + 1) * 128],
                                      start=(h % 4 == 0), stop=(h % 4 == 3),
                                      skip_group_check=True)
                            ins = tt.matmul(ksum_ps[:, h:h + 1],
                                            k_sb[:, h * 128:(h + 1) * 128],
                                            ones_col[:, 0:1],
                                            start=(t == 0 and h == 0),
                                            stop=(t == N1 - 1 and h == H - 1),
                                            skip_group_check=True)
                            if h == H - 1:
                                ins.then_inc(S.pe1, 1)

                @blk.scalar
                def _(s):
                    for t in range(N1):
                        for ob in range(4):
                            gidx = 4 * t + ob
                            s.wait_ge(S.kg1, gidx + 1)
                            if gidx > 0:
                                s.wait_ge(S.ev1, gidx)
                            s.activation(e1_sb[:, :], k_ps[:, :], AF.Exp).then_inc(S.sc1, 1)
                            s.activation(r1_sb[:, :], k_ps[:, :], AF.Relu).then_inc(S.sc1, 1)
                            s.wait_ge(S.vg1, gidx + 1)
                            if ob == 0 and t > 0:
                                s.wait_ge(S.pe1, t)
                            s.copy(v_sb[:, ob * 512:(ob + 1) * 512],
                                   v_ps[:, :]).then_inc(S.vc1, 1)

                @blk.vector
                def _(v):
                    v.memset(kvpack[:, :], 0.0)
                    for t in range(N1):
                        for ob in range(4):
                            gidx = 4 * t + ob
                            if ob == 0 and t > 0:
                                v.wait_ge(S.pe1, t)
                            v.wait_ge(S.sc1, 2 * gidx + 2)
                            v.scalar_tensor_tensor(k_sb[:, ob * 512:(ob + 1) * 512],
                                                   e1_sb[:, :], 1.0, r1_sb[:, :],
                                                   ALU.min, ALU.add).then_inc(S.ev1, 1)
                        v.wait_ge(S.pe1, t + 1)
                        v.tensor_tensor(kvpack[:, 0:2048], kvpack[:, 0:2048],
                                        kv_ps[:, :], ALU.add).then_inc(S.kvf, 1)
                    v.tensor_copy(kvpack[:, 2048:2064], ksum_ps[:, :]).then_inc(S.kse, 1)

        with ExitStack() as st2:
            wq_sb = st2.enter_context(nc.sbuf_tensor("wq_sb", [128, NCH, 2048], BF16))
            wo_sb = st2.enter_context(nc.sbuf_tensor("wo_sb", [128, NCH, 2048], BF16))
            xb_sb = st2.enter_context(nc.sbuf_tensor("xb_sb", [128, NCH, T2], BF16))
            q_sb = st2.enter_context(nc.sbuf_tensor("q_sb", [128, NCH, T2], BF16))
            e2_sb = st2.enter_context(nc.sbuf_tensor("e2_sb", [128, T2], F32))
            r2_sb = st2.enter_context(nc.sbuf_tensor("r2_sb", [128, T2], F32))
            num_sb = st2.enter_context(nc.sbuf_tensor("num_sb", [128, 2, T2], BF16))
            zb_sb = st2.enter_context(nc.sbuf_tensor("zb_sb", [128, 2, T2], F32))
            xf_sb = st2.enter_context(nc.sbuf_tensor("xf_sb", [128, 2, T2], F32))
            y1st_sb = st2.enter_context(nc.sbuf_tensor("y1st_sb", [128, 2, T2], F32))
            y1bf_sb = st2.enter_context(nc.sbuf_tensor("y1bf_sb", [128, 2, T2], BF16))
            sqbf_sb = st2.enter_context(nc.sbuf_tensor("sqbf_sb", [128, 2, T2], BF16))
            z0_sb = st2.enter_context(nc.sbuf_tensor("z0_sb", [16, T2], F32))
            zhi_sb = st2.enter_context(nc.sbuf_tensor("zhi_sb", [128, T2], BF16))
            zlo_sb = st2.enter_context(nc.sbuf_tensor("zlo_sb", [128, T2], BF16))
            mr_sb = st2.enter_context(nc.sbuf_tensor("mr_sb", [1, 2 * T2], F32))
            m2_sb = st2.enter_context(nc.sbuf_tensor("m2_sb", [1, T2], F32))
            var_sb = st2.enter_context(nc.sbuf_tensor("var_sb", [1, T2], F32))
            qn_ps = [st2.enter_context(nc.psum_tensor(f"qn_ps{i}", [128, T2], F32)) for i in range(2)]
            dsq_ps = st2.enter_context(nc.psum_tensor("dsq_ps", [16, T2], F32))
            zb_ps = [st2.enter_context(nc.psum_tensor(f"zb_ps{i}", [128, T2], F32)) for i in range(2)]
            at_ps = [st2.enter_context(nc.psum_tensor(f"at_ps{i}", [128, T2], F32)) for i in range(2)]
            ssum_ps = st2.enter_context(nc.psum_tensor("ssum_ps", [1, T2], F32))

            xTr2 = xT.ap().rearrange("(cc p) t -> p cc t", p=128)
            wqr = wqT.ap().rearrange("(cc p) o -> p cc o", p=128)
            wor = woT.ap().rearrange("(cc p) o -> p cc o", p=128)
            wqsem = [S.wqa, S.wqb]
            wosem = [S.woa, S.wob]
            xfsem = [S.xfa, S.xfb]
            y1sem = [S.y1sa, S.y1sb]

            with nc.Block("sweep2") as blk:
                @blk.gpsimd
                def _(g):
                    g.collective_compute(
                        "AllReduce", ALU.add,
                        replica_groups=[[0, 1], [2, 3], [4, 5], [6, 7]],
                        ins=[ccin.ap().opt()], outs=[ccout.ap().opt()],
                    ).then_inc(S.ccd, 1)
                    g.dma_start(out=xb_sb[:, :, :], in_=xTr2[:, :, 0:T2]).then_inc(S.xb2, 16)
                    for h in range(NCH):
                        g.dma_start(out=wq_sb[:, :, h * 128:(h + 1) * 128],
                                    in_=wqr[:, :, h * 128:(h + 1) * 128]).then_inc(wqsem[h % 2], 16)
                    for o in range(NCH):
                        g.dma_start(out=wo_sb[:, :, o * 128:(o + 1) * 128],
                                    in_=wor[:, :, o * 128:(o + 1) * 128]).then_inc(wosem[o % 2], 16)
                    g.wait_ge(S.ccd, 1)
                    g.dma_start(out=kv_bf[:, :], in_=ccout.ap()[:, 0:2048]).then_inc(S.ccr, 16)
                    g.dma_start(out=ksumf[:, :], in_=ccout.ap()[:, 2048:2064]).then_inc(S.ccr, 16)
                    for j in range(1, N2):
                        g.wait_ge(S.qd2, 16 * j)
                        g.dma_start(out=xb_sb[:, :, :],
                                    in_=xTr2[:, :, j * T2:(j + 1) * T2]).then_inc(S.xb2, 16)

                @blk.tensor
                def _(tt):
                    for j in range(N2):
                        for h in range(NCH):
                            gq = NCH * j + h
                            if j == 0:
                                tt.wait_ge(wqsem[h % 2], 16 * (h // 2 + 1))
                            if h == 0:
                                tt.wait_ge(S.xb2, 16 * (j + 1))
                                if j > 0:
                                    tt.wait_ge(S.numc2, 16 * j)
                            if gq >= 2:
                                tt.wait_ge(S.sc2, 2 * (gq - 1))
                            for c in range(NCH):
                                ins = tt.matmul(qn_ps[h % 2][:, :],
                                                wq_sb[:, c, h * 128:(h + 1) * 128],
                                                xb_sb[:, c, :],
                                                start=(c == 0), stop=(c == NCH - 1))
                                if c == NCH - 1:
                                    ins.then_inc(S.qd2, 1)
                        if j == 0:
                            tt.wait_ge(S.dl2, 1)
                        tt.wait_ge(S.qe2, NCH * (j + 1))
                        if j > 0:
                            tt.wait_ge(S.zd2, 4 * j)
                            tt.wait_ge(S.msv2, 3 * j)
                        for h in range(NCH):
                            ins = tt.matmul(dsq_ps[0:16, :], dl_bf[:, h * 16:(h + 1) * 16],
                                            q_sb[:, h, :], start=(h == 0), stop=(h == NCH - 1),
                                            skip_group_check=True)
                            if h == NCH - 1:
                                ins.then_inc(S.dend2, 1)
                        for h in range(NCH):
                            gq = NCH * j + h
                            if h == 0:
                                tt.wait_ge(S.zd2, 4 * (j + 1))
                                tt.wait_ge(S.sc2, 32 * (j + 1))
                                if j == 0:
                                    tt.wait_ge(S.ccr, 32)
                            if gq >= 2:
                                tt.wait_ge(S.numc2, gq - 1)
                            tt.matmul(qn_ps[h % 2][:, :], kv_bf[:, h * 128:(h + 1) * 128],
                                      q_sb[:, h, :], start=True, stop=True).then_inc(S.num2, 1)
                            if gq >= 2:
                                tt.wait_ge(S.zbc2, gq - 1)
                            tt.matmul(zb_ps[h % 2][:, :], sel_bf[:, h * 128:(h + 1) * 128],
                                      zhi_sb[:, :], start=True, stop=False,
                                      skip_group_check=True)
                            tt.matmul(zb_ps[h % 2][:, :], sel_bf[:, h * 128:(h + 1) * 128],
                                      zlo_sb[:, :], start=False, stop=True,
                                      skip_group_check=True).then_inc(S.zbm2, 1)
                        def stats(op):
                            tt.wait_ge(S.cst2, 2 * (NCH * j + op + 1))
                            if op == 0:
                                tt.wait_ge(S.zd2, 4 * j + 1)
                                if j > 0:
                                    tt.wait_ge(S.msv2, 3 * j)
                            tt.matmul(ssum_ps[0:1, :], ones_col[:, 0:1],
                                      y1bf_sb[:, op % 2, :], start=(op == 0),
                                      stop=(op == NCH - 1), skip_group_check=True)
                            ins = tt.matmul(dsq_ps[0:1, :], ones_col[:, 0:1],
                                            sqbf_sb[:, op % 2, :], start=(op == 0),
                                            stop=(op == NCH - 1), skip_group_check=True)
                            ins.then_inc(S.stt2, 1)
                        for o in range(NCH):
                            go = NCH * j + o
                            if j == 0:
                                tt.wait_ge(wosem[o % 2], 16 * (o // 2 + 1))
                            if o == 0:
                                tt.wait_ge(S.mx2, NCH * (j + 1))
                            if go >= 2:
                                tt.wait_ge(S.y12, go - 1)
                            for c in range(NCH):
                                ins = tt.matmul(at_ps[o % 2][:, :],
                                                wo_sb[:, c, o * 128:(o + 1) * 128],
                                                q_sb[:, c, :],
                                                start=(c == 0), stop=(c == NCH - 1))
                                if c == NCH - 1:
                                    ins.then_inc(S.attn2, 1)
                            if o >= 1:
                                stats(o - 1)
                        stats(NCH - 1)

                @blk.scalar
                def _(s):
                    for j in range(N2):
                        for h in range(NCH):
                            gq = NCH * j + h
                            s.wait_ge(S.qd2, gq + 1)
                            if gq >= 1:
                                s.wait_ge(S.qe2, gq)
                            s.activation(e2_sb[:, :], qn_ps[h % 2][:, :], AF.Exp).then_inc(S.sc2, 1)
                            s.activation(r2_sb[:, :], qn_ps[h % 2][:, :], AF.Relu).then_inc(S.sc2, 1)
                        for h in range(NCH):
                            gq = NCH * j + h
                            s.wait_ge(S.num2, gq + 1)
                            if gq >= 2:
                                s.wait_ge(S.mx2, gq - 1)
                            s.copy(num_sb[:, h % 2, :], qn_ps[h % 2][:, :]).then_inc(S.numc2, 1)
                            s.wait_ge(S.zbm2, gq + 1)
                            s.copy(zb_sb[:, h % 2, :], zb_ps[h % 2][:, :]).then_inc(S.zbc2, 1)
                        for o in range(NCH):
                            go = NCH * j + o
                            s.wait_ge(S.y12, go + 1)
                            if go >= 2:
                                s.wait_ge(S.stt2, go - 1)
                            s.copy(y1bf_sb[:, o % 2, :], y1st_sb[:, o % 2, :]).then_inc(S.cst2, 1)
                            s.activation(sqbf_sb[:, o % 2, :], y1st_sb[:, o % 2, :],
                                         AF.Square).then_inc(S.cst2, 1)
                        s.wait_ge(S.msv2, 3 * (j + 1))
                        if j > 0:
                            s.wait_ge(S.msd2, 16 * j)
                        s.activation(m2_sb[0:1, :], var_sb[0:1, :], AF.Ln, bias=1e-5)
                        s.activation(mr_sb[0:1, T2:2 * T2], m2_sb[0:1, :], AF.Exp,
                                     scale=-0.5).then_inc(S.rs2, 1)

                @blk.vector
                def _(v):
                    for j in range(N2):
                        for h in range(NCH):
                            gq = NCH * j + h
                            v.wait_ge(S.sc2, 2 * gq + 2)
                            v.scalar_tensor_tensor(q_sb[:, h, :], e2_sb[:, :], 1.0,
                                                   r2_sb[:, :], ALU.min, ALU.add).then_inc(S.qe2, 1)
                        if j == 0:
                            v.wait_ge(S.ccr, 32)
                            for h in range(H):
                                ins = v.tensor_copy(dl_bf[:, h * 16 + h:h * 16 + h + 1],
                                                    ksumf[:, h:h + 1])
                                if h == H - 1:
                                    ins.then_inc(S.dl2, 1)
                        if j == 0:
                            v.memset(zhi_sb[:, :], 0.0)
                            v.memset(zlo_sb[:, :], 0.0)
                        v.wait_ge(S.dend2, j + 1)
                        v.tensor_scalar_add(z0_sb[0:16, :], dsq_ps[0:16, :], 1e-6).then_inc(S.zd2, 1)
                        v.reciprocal(z0_sb[0:16, :], z0_sb[0:16, :]).then_inc(S.zd2, 1)
                        v.tensor_copy(zhi_sb[0:16, :], z0_sb[0:16, :]).then_inc(S.zd2, 1)
                        v.tensor_tensor(zlo_sb[0:16, :], z0_sb[0:16, :], zhi_sb[0:16, :],
                                        ALU.subtract).then_inc(S.zd2, 1)
                        for h in range(NCH):
                            gq = NCH * j + h
                            v.wait_ge(S.zbc2, gq + 1)
                            v.wait_ge(S.numc2, gq + 1)
                            v.tensor_tensor(q_sb[:, h, :], num_sb[:, h % 2, :],
                                            zb_sb[:, h % 2, :], ALU.mult).then_inc(S.mx2, 1)
                        for o in range(NCH):
                            go = NCH * j + o
                            v.wait_ge(S.attn2, go + 1)
                            v.wait_ge(xfsem[o % 2], 16 * (8 * j + o // 2 + 1))
                            if 8 * j + o // 2 >= 1:
                                v.wait_ge(y1sem[o % 2], 16 * (8 * j + o // 2))
                            if go >= 2:
                                v.wait_ge(S.cst2, 2 * (go - 1))
                            v.tensor_tensor(y1st_sb[:, o % 2, :], xf_sb[:, o % 2, :],
                                            at_ps[o % 2][:, :], ALU.add).then_inc(S.y12, 1)
                        v.wait_ge(S.stt2, NCH * (j + 1))
                        if j > 0:
                            v.wait_ge(S.msd2, 16 * j)
                        v.tensor_scalar_mul(mr_sb[0:1, 0:T2], ssum_ps[0:1, :],
                                            1.0 / 2048.0).then_inc(S.msv2, 1)
                        v.tensor_tensor(m2_sb[0:1, :], mr_sb[0:1, 0:T2], mr_sb[0:1, 0:T2],
                                        ALU.mult).then_inc(S.msv2, 1)
                        v.scalar_tensor_tensor(var_sb[0:1, :], dsq_ps[0:1, :], 1.0 / 2048.0,
                                               m2_sb[0:1, :], ALU.mult,
                                               ALU.subtract).then_inc(S.msv2, 1)

                @blk.sync
                def _(sy):
                    sy.wait_ge(S.ccs, 16)
                    for j in range(N2):
                        def xf_load(o):
                            go = NCH * j + o
                            if go >= 2:
                                sy.wait_ge(S.y12, go - 1)
                            sy.dma_start(out=xf_sb[:, o % 2, :],
                                         in_=xT.ap()[o * 128:(o + 1) * 128,
                                                     j * T2:(j + 1) * T2]).then_inc(xfsem[o % 2], 16)
                        xf_load(0)
                        xf_load(1)
                        for o in range(NCH):
                            sy.wait_ge(S.y12, NCH * j + o + 1)
                            sy.dma_start(out=y1raw_d.ap()[o * 128:(o + 1) * 128,
                                                          j * T2:(j + 1) * T2],
                                         in_=y1st_sb[:, o % 2, :]).then_inc(y1sem[o % 2], 16)
                            if o + 2 < NCH:
                                xf_load(o + 2)
                        sy.wait_ge(S.rs2, j + 1)
                        sy.dma_start(out=mstd_d.ap()[j:j + 1, :],
                                     in_=mr_sb[0:1, :]).then_inc(S.msd2, 16)
                    sy.wait_ge(S.y1sa, 16 * 8 * N2)
                    sy.wait_ge(S.y1sb, 16 * 8 * N2)
                    sy.wait_ge(S.msd2, 16 * N2)

        with ExitStack() as st3:
            w1_sb = st3.enter_context(nc.sbuf_tensor("w1_sb", [128, NCH, 2048], BF16))
            vall_sb = st3.enter_context(nc.sbuf_tensor("vall_sb", [128, NCH, RTOT], BF16))
            uall_sb = st3.enter_context(nc.sbuf_tensor("uall_sb", [128, 4, 2048], BF16))
            y1f_sb = st3.enter_context(nc.sbuf_tensor("y1f_sb", [128, 2, NCH, T3], F32))
            y1b3_sb = st3.enter_context(nc.sbuf_tensor("y1b3_sb", [128, NCH, T3], BF16))
            h_sb = st3.enter_context(nc.sbuf_tensor("h_sb", [128, NCH, T3], BF16))
            vxbf_sb = st3.enter_context(nc.sbuf_tensor("vxbf_sb", [128, 2, T3], BF16))
            ecp_sb = st3.enter_context(nc.sbuf_tensor("ecp_sb", [128, 2, T3], F32))
            rcb_sb = st3.enter_context(nc.sbuf_tensor("rcb_sb", [8, T3], F32))
            vxs_sb = st3.enter_context(nc.sbuf_tensor("vxs_sb", [128, 4, T3], BF16))
            y2raw_sb = st3.enter_context(nc.sbuf_tensor("y2raw_sb", [128, NCH, T3], F32))
            outst_sb = st3.enter_context(nc.sbuf_tensor("outst_sb", [128, 4, T3], F32))
            y2bf_sb = st3.enter_context(nc.sbuf_tensor("y2bf_sb", [128, 2, T3], BF16))
            sq2_sb = st3.enter_context(nc.sbuf_tensor("sq2_sb", [128, 2, T3], BF16))
            mr1_sb = st3.enter_context(nc.sbuf_tensor("mr1_sb", [1, 2, 512], F32))
            m1hi = st3.enter_context(nc.sbuf_tensor("m1hi", [1, T3], BF16))
            m1lo = st3.enter_context(nc.sbuf_tensor("m1lo", [1, T3], BF16))
            r1hi = st3.enter_context(nc.sbuf_tensor("r1hi", [1, T3], BF16))
            r1lo = st3.enter_context(nc.sbuf_tensor("r1lo", [1, T3], BF16))
            lg_sb = st3.enter_context(nc.sbuf_tensor("lg_sb", [8, T3], F32))
            expE_sb = st3.enter_context(nc.sbuf_tensor("expE_sb", [8, T3], BF16))
            alpha_sb = st3.enter_context(nc.sbuf_tensor("alpha_sb", [128, T3], BF16))
            rcp_sb = st3.enter_context(nc.sbuf_tensor("rcp_sb", [1, T3], F32))
            rcphi = st3.enter_context(nc.sbuf_tensor("rcphi", [1, T3], BF16))
            rcplo = st3.enter_context(nc.sbuf_tensor("rcplo", [1, T3], BF16))
            m2m_sb = st3.enter_context(nc.sbuf_tensor("m2m_sb", [1, T3], F32))
            m2sq_sb = st3.enter_context(nc.sbuf_tensor("m2sq_sb", [1, T3], F32))
            var2_sb = st3.enter_context(nc.sbuf_tensor("var2_sb", [1, T3], F32))
            lnv2_sb = st3.enter_context(nc.sbuf_tensor("lnv2_sb", [1, T3], F32))
            rstd2_sb = st3.enter_context(nc.sbuf_tensor("rstd2_sb", [1, T3], F32))
            m2hi = st3.enter_context(nc.sbuf_tensor("m2hi", [1, T3], BF16))
            m2lo = st3.enter_context(nc.sbuf_tensor("m2lo", [1, T3], BF16))
            r2hi = st3.enter_context(nc.sbuf_tensor("r2hi", [1, T3], BF16))
            r2lo = st3.enter_context(nc.sbuf_tensor("r2lo", [1, T3], BF16))
            hv_ps = [st3.enter_context(nc.psum_tensor(f"hv_ps{i}", [128, T3], F32)) for i in range(2)]
            lgd_ps = st3.enter_context(nc.psum_tensor("lgd_ps", [8, T3], F32))
            rcpb_ps = st3.enter_context(nc.psum_tensor("rcpb_ps", [8, T3], F32))
            e_ps = [st3.enter_context(nc.psum_tensor(f"e_ps{i}", [128, T3], F32)) for i in range(2)]
            m1b_ps = st3.enter_context(nc.psum_tensor("m1b_ps", [128, T3], F32))
            r1b_ps = st3.enter_context(nc.psum_tensor("r1b_ps", [128, T3], F32))

            y1r = y1raw_d.ap().rearrange("(cc p) t -> p cc t", p=128)
            w1r = w1.ap().rearrange("(dc p) e -> p dc e", p=128)
            vr = vallT.ap().rearrange("(dc p) r -> p dc r", p=128)
            ur = uallT.ap().rearrange("(rc p) d -> p rc d", p=128)
            y3sem = [S.y3a, S.y3b]
            mrsem = [S.mra, S.mrb]
            w1sem = [S.w1a, S.w1b]
            o3sem = [S.o3a, S.o3b, S.o3c, S.o3d]

            with nc.Block("sweep3") as blk:
                @blk.gpsimd
                def _(g):
                    def y1_load(b):
                        g.dma_start(out=y1f_sb[:, b % 2, :, :],
                                    in_=y1r[:, :, b * T3:(b + 1) * T3]).then_inc(y3sem[b % 2], 16)

                    def mr_load(b):
                        j2, hf = b // 2, b % 2
                        g.dma_start(out=mr1_sb[0:1, b % 2, 0:T3],
                                    in_=mstd_d.ap()[j2:j2 + 1,
                                                    hf * T3:hf * T3 + T3]).then_inc(mrsem[b % 2], 16)
                        g.dma_start(out=mr1_sb[0:1, b % 2, 256:256 + T3],
                                    in_=mstd_d.ap()[j2:j2 + 1,
                                                    T2 + hf * T3:T2 + hf * T3 + T3]).then_inc(mrsem[b % 2], 16)
                    y1_load(0)
                    mr_load(0)
                    g.dma_start(out=w1_sb[:, :, 0:128], in_=w1r[:, :, 0:128]).then_inc(S.w1a, 16)
                    g.dma_start(out=w1_sb[:, :, 128:256], in_=w1r[:, :, 128:256]).then_inc(S.w1b, 16)
                    y1_load(1)
                    mr_load(1)
                    g.dma_start(out=vall_sb[:, :, :], in_=vr[:, :, :]).then_inc(S.wvu, 16)
                    g.dma_start(out=uall_sb[:, :, :], in_=ur[:, :, :]).then_inc(S.wvu, 16)
                    for oc in range(2, NCH):
                        g.wait_ge(w1sem[oc % 2], 16 * (oc // 2))
                        g.dma_start(out=w1_sb[:, :, oc * 128:(oc + 1) * 128],
                                    in_=w1r[:, :, oc * 128:(oc + 1) * 128]).then_inc(w1sem[oc % 2], 16)
                    for b in range(2, N3):
                        g.wait_ge(S.ue3, 16 * (b - 1))
                        y1_load(b)
                        g.wait_ge(S.hl3, 4 * (b - 1))
                        mr_load(b)

                @blk.tensor
                def _(tt):
                    for b in range(N3):
                        tt.wait_ge(S.hl3, 4 * (b + 1))
                        if b > 0:
                            tt.wait_ge(S.n23, 32 * b)
                        tt.matmul(m1b_ps[:, :], ones_row[0:1, :], m1hi[0:1, :],
                                  start=True, stop=False, skip_group_check=True)
                        tt.matmul(m1b_ps[:, :], ones_row[0:1, :], m1lo[0:1, :],
                                  start=False, stop=True, skip_group_check=True).then_inc(S.bc3, 1)
                        tt.matmul(r1b_ps[:, :], ones_row[0:1, :], r1hi[0:1, :],
                                  start=True, stop=False, skip_group_check=True)
                        tt.matmul(r1b_ps[:, :], ones_row[0:1, :], r1lo[0:1, :],
                                  start=False, stop=True, skip_group_check=True).then_inc(S.bc3, 1)
                        for oc in range(NCH):
                            g3 = NCH * b + oc
                            if b == 0:
                                tt.wait_ge(w1sem[oc % 2], 16 * (oc // 2 + 1))
                            if oc == 0:
                                tt.wait_ge(S.yc3, NCH * (b + 1))
                            if g3 >= 2:
                                tt.wait_ge(S.hr3, g3 - 1)
                            for dc in range(NCH):
                                ins = tt.matmul(hv_ps[oc % 2][:, :],
                                                w1_sb[:, dc, oc * 128:(oc + 1) * 128],
                                                y1b3_sb[:, dc, :],
                                                start=(dc == 0), stop=(dc == NCH - 1))
                                if dc == NCH - 1:
                                    ins.then_inc(S.hg3, 1)
                        tt.wait_ge(S.hr3, NCH * (b + 1))
                        if b == 0:
                            tt.wait_ge(S.ic1, 48)
                        else:
                            tt.wait_ge(S.mv3, 7 * b - 4)
                        for ec in range(NCH):
                            ins = tt.matmul(lgd_ps[0:8, :], w2s[:, ec, :], h_sb[:, ec, :],
                                            start=(ec == 0), stop=(ec == NCH - 1),
                                            skip_group_check=True)
                            if ec == NCH - 1:
                                ins.then_inc(S.lg3, 1)
                        tt.wait_ge(S.se3, b + 1)
                        tt.wait_ge(S.sm3, 2 * (b + 1))
                        tt.matmul(lgd_ps[0:1, :], ones8[0:8, 0:1], expE_sb[0:8, :],
                                  start=True, stop=True, skip_group_check=True).then_inc(S.ds3, 1)
                        tt.wait_ge(S.rc3, 3 * (b + 1))
                        if b > 0:
                            tt.wait_ge(S.rcb3, b)
                        tt.matmul(rcpb_ps[0:8, :], ones_row[0:1, 0:8], rcphi[0:1, :],
                                  start=True, stop=False, skip_group_check=True)
                        tt.matmul(rcpb_ps[0:8, :], ones_row[0:1, 0:8], rcplo[0:1, :],
                                  start=False, stop=True, skip_group_check=True).then_inc(S.rb3, 1)
                        for rc in range(4):
                            g4 = 4 * b + rc
                            if rc == 0:
                                tt.wait_ge(S.al3, b + 1)
                                if b == 0:
                                    tt.wait_ge(S.wvu, 32)
                            if g4 >= 2:
                                tt.wait_ge(S.ec3, g4 - 1)
                            tt.matmul(e_ps[rc % 2][:, :], m1sel[:, rc * 128:(rc + 1) * 128],
                                      alpha_sb[:, :], start=True, stop=True).then_inc(S.eg3, 1)
                            if g4 >= 2:
                                tt.wait_ge(S.vc3, g4 - 1)
                            for dc in range(NCH):
                                ins = tt.matmul(hv_ps[rc % 2][:, :],
                                                vall_sb[:, dc, rc * 128:(rc + 1) * 128],
                                                y1b3_sb[:, dc, :],
                                                start=(dc == 0), stop=(dc == NCH - 1))
                                if dc == NCH - 1:
                                    ins.then_inc(S.vx3, 1)
                        def stats2(op):
                            tt.wait_ge(S.c23, 2 * (NCH * b + op + 1))
                            tt.matmul(rcpb_ps[0:1, :], ones_col[:, 0:1],
                                      y2bf_sb[:, op % 2, :], start=(op == 0),
                                      stop=(op == NCH - 1), skip_group_check=True)
                            ins = tt.matmul(lgd_ps[0:1, :], ones_col[:, 0:1],
                                            sq2_sb[:, op % 2, :], start=(op == 0),
                                            stop=(op == NCH - 1), skip_group_check=True)
                            ins.then_inc(S.st3, 1)
                        for oc in range(NCH):
                            g3 = NCH * b + oc
                            if oc == 0:
                                tt.wait_ge(S.vs3, 4 * (b + 1))
                                tt.wait_ge(S.vc3, 4 * (b + 1))
                            if g3 >= 2:
                                tt.wait_ge(S.ue3, g3 - 1)
                            for rc in range(4):
                                ins = tt.matmul(hv_ps[oc % 2][:, :],
                                                uall_sb[:, rc, oc * 128:(oc + 1) * 128],
                                                vxs_sb[:, rc, :],
                                                start=(rc == 0), stop=(rc == 3))
                                if rc == 3:
                                    ins.then_inc(S.ug3, 1)
                            if oc >= 1:
                                stats2(oc - 1)
                        stats2(NCH - 1)
                        tt.wait_ge(S.mv3, 7 * (b + 1))
                        tt.wait_ge(S.nr3, 32 * (b + 1))
                        tt.matmul(m1b_ps[:, :], ones_row[0:1, :], m2hi[0:1, :],
                                  start=True, stop=False, skip_group_check=True)
                        tt.matmul(m1b_ps[:, :], ones_row[0:1, :], m2lo[0:1, :],
                                  start=False, stop=True, skip_group_check=True).then_inc(S.b23, 1)
                        tt.matmul(r1b_ps[:, :], ones_row[0:1, :], r2hi[0:1, :],
                                  start=True, stop=False, skip_group_check=True)
                        tt.matmul(r1b_ps[:, :], ones_row[0:1, :], r2lo[0:1, :],
                                  start=False, stop=True, skip_group_check=True).then_inc(S.b23, 1)

                @blk.scalar
                def _(s):
                    for b in range(N3):
                        for cc in range(NCH):
                            s.wait_ge(S.nr3, 32 * b + 2 * (cc + 1))
                            if cc == 0 and b > 0:
                                s.wait_ge(S.vx3, 4 * b)
                            s.copy(y1b3_sb[:, cc, :], y1f_sb[:, b % 2, cc, :]).then_inc(S.yc3, 1)
                        for oc in range(NCH):
                            g3 = NCH * b + oc
                            s.wait_ge(S.hg3, g3 + 1)
                            if b == 0 and oc == 0:
                                s.wait_ge(S.ic1, 48)
                            s.activation(h_sb[:, oc, :], hv_ps[oc % 2][:, :], AF.Relu,
                                         bias=b1c[:, oc:oc + 1]).then_inc(S.hr3, 1)
                        s.wait_ge(S.sm3, 2 * (b + 1))
                        s.activation(expE_sb[0:8, :], lg_sb[0:8, :], AF.Exp).then_inc(S.se3, 1)
                        s.wait_ge(S.rb3, b + 1)
                        if b > 0:
                            s.wait_ge(S.al3, b)
                        s.copy(rcb_sb[0:8, :], rcpb_ps[0:8, :]).then_inc(S.rcb3, 1)
                        for rc in range(4):
                            g4 = 4 * b + rc
                            s.wait_ge(S.vx3, g4 + 1)
                            if g4 >= 2:
                                s.wait_ge(S.vs3, g4 - 1)
                            s.copy(vxbf_sb[:, rc % 2, :], hv_ps[rc % 2][:, :]).then_inc(S.vc3, 1)
                            s.wait_ge(S.eg3, g4 + 1)
                            s.copy(ecp_sb[:, rc % 2, :], e_ps[rc % 2][:, :]).then_inc(S.ec3, 1)
                        for oc in range(NCH):
                            g3 = NCH * b + oc
                            s.wait_ge(S.ue3, g3 + 1)
                            if g3 >= 2:
                                s.wait_ge(S.st3, g3 - 1)
                            s.copy(y2bf_sb[:, oc % 2, :], y2raw_sb[:, oc, :]).then_inc(S.c23, 1)
                            s.activation(sq2_sb[:, oc % 2, :], y2raw_sb[:, oc, :],
                                         AF.Square).then_inc(S.c23, 1)
                        s.wait_ge(S.mv3, 7 * b + 3)
                        s.activation(lnv2_sb[0:1, :], var2_sb[0:1, :], AF.Ln, bias=1e-5)
                        s.activation(rstd2_sb[0:1, :], lnv2_sb[0:1, :], AF.Exp,
                                     scale=-0.5).then_inc(S.r23, 1)

                @blk.vector
                def _(v):
                    for b in range(N3):
                        v.wait_ge(mrsem[b % 2], 32 * (b // 2 + 1))
                        if b > 0:
                            v.wait_ge(S.bc3, 2 * b)
                        v.tensor_copy(m1hi[0:1, :], mr1_sb[0:1, b % 2, 0:T3]).then_inc(S.hl3, 1)
                        v.tensor_tensor(m1lo[0:1, :], mr1_sb[0:1, b % 2, 0:T3], m1hi[0:1, :],
                                        ALU.subtract).then_inc(S.hl3, 1)
                        v.tensor_copy(r1hi[0:1, :], mr1_sb[0:1, b % 2, 256:256 + T3]).then_inc(S.hl3, 1)
                        v.tensor_tensor(r1lo[0:1, :], mr1_sb[0:1, b % 2, 256:256 + T3],
                                        r1hi[0:1, :], ALU.subtract).then_inc(S.hl3, 1)
                        for cc in range(NCH):
                            if cc == 0:
                                v.wait_ge(S.bc3, 2 * (b + 1))
                                v.wait_ge(y3sem[b % 2], 16 * (b // 2 + 1))
                            v.tensor_tensor(y1f_sb[:, b % 2, cc, :], y1f_sb[:, b % 2, cc, :],
                                            m1b_ps[:, :], ALU.subtract).then_inc(S.nr3, 1)
                            v.tensor_tensor(y1f_sb[:, b % 2, cc, :], y1f_sb[:, b % 2, cc, :],
                                            r1b_ps[:, :], ALU.mult).then_inc(S.nr3, 1)
                        if b == 0:
                            v.memset(alpha_sb[:, :], 0.0)
                        v.wait_ge(S.lg3, b + 1)
                        v.tensor_scalar(lg_sb[0:8, :], lgd_ps[0:8, :], b2c[0:8, 0:1], 50.0,
                                        ALU.add, ALU.min).then_inc(S.sm3, 1)
                        v.tensor_scalar_max(lg_sb[0:8, :], lg_sb[0:8, :], -50.0).then_inc(S.sm3, 1)
                        v.wait_ge(S.ds3, b + 1)
                        v.reciprocal(rcp_sb[0:1, :], lgd_ps[0:1, :]).then_inc(S.rc3, 1)
                        v.tensor_copy(rcphi[0:1, :], rcp_sb[0:1, :]).then_inc(S.rc3, 1)
                        v.tensor_tensor(rcplo[0:1, :], rcp_sb[0:1, :], rcphi[0:1, :],
                                        ALU.subtract).then_inc(S.rc3, 1)
                        v.wait_ge(S.rcb3, b + 1)
                        v.wait_ge(S.se3, b + 1)
                        v.tensor_tensor(alpha_sb[0:8, :], expE_sb[0:8, :], rcb_sb[0:8, :],
                                        ALU.mult).then_inc(S.al3, 1)
                        for rc in range(4):
                            g4 = 4 * b + rc
                            v.wait_ge(S.vc3, g4 + 1)
                            v.wait_ge(S.ec3, g4 + 1)
                            v.tensor_tensor(vxs_sb[:, rc, :], vxbf_sb[:, rc % 2, :],
                                            ecp_sb[:, rc % 2, :], ALU.mult).then_inc(S.vs3, 1)
                        for oc in range(NCH):
                            g3 = NCH * b + oc
                            v.wait_ge(S.ug3, g3 + 1)
                            v.tensor_tensor(y2raw_sb[:, oc, :], y1f_sb[:, b % 2, oc, :],
                                            hv_ps[oc % 2][:, :], ALU.add).then_inc(S.ue3, 1)
                        v.wait_ge(S.st3, NCH * (b + 1))
                        v.tensor_scalar_mul(m2m_sb[0:1, :], rcpb_ps[0:1, :],
                                            1.0 / 2048.0).then_inc(S.mv3, 1)
                        v.tensor_tensor(m2sq_sb[0:1, :], m2m_sb[0:1, :], m2m_sb[0:1, :],
                                        ALU.mult).then_inc(S.mv3, 1)
                        v.scalar_tensor_tensor(var2_sb[0:1, :], lgd_ps[0:1, :], 1.0 / 2048.0,
                                               m2sq_sb[0:1, :], ALU.mult,
                                               ALU.subtract).then_inc(S.mv3, 1)
                        v.wait_ge(S.r23, b + 1)
                        if b > 0:
                            v.wait_ge(S.b23, 2 * b)
                        v.tensor_copy(m2hi[0:1, :], m2m_sb[0:1, :]).then_inc(S.mv3, 1)
                        v.tensor_tensor(m2lo[0:1, :], m2m_sb[0:1, :], m2hi[0:1, :],
                                        ALU.subtract).then_inc(S.mv3, 1)
                        v.tensor_copy(r2hi[0:1, :], rstd2_sb[0:1, :]).then_inc(S.mv3, 1)
                        v.tensor_tensor(r2lo[0:1, :], rstd2_sb[0:1, :], r2hi[0:1, :],
                                        ALU.subtract).then_inc(S.mv3, 1)
                        for oc in range(NCH):
                            if oc == 0:
                                v.wait_ge(S.b23, 2 * (b + 1))
                            v.tensor_tensor(y2raw_sb[:, oc, :], y2raw_sb[:, oc, :],
                                            m1b_ps[:, :], ALU.subtract).then_inc(S.n23, 1)
                            if 4 * b + oc // 4 >= 1:
                                v.wait_ge(o3sem[oc % 4], 16 * (4 * b + oc // 4))
                            v.tensor_tensor(outst_sb[:, oc % 4, :], y2raw_sb[:, oc, :],
                                            r1b_ps[:, :], ALU.mult).then_inc(S.n23, 1)

                @blk.sync
                def _(sy):
                    for b in range(N3):
                        for oc in range(NCH):
                            sy.wait_ge(S.n23, 32 * b + 2 * (oc + 1))
                            sy.dma_start(out=outT.ap()[oc * 128:(oc + 1) * 128,
                                                       b * T3:(b + 1) * T3],
                                         in_=outst_sb[:, oc % 4, :]).then_inc(o3sem[oc % 4], 16)
                    for s4 in o3sem:
                        sy.wait_ge(s4, 16 * 4 * N3)

    st.close()
    return nc


_NC_CACHE = []


def kernel(L0_attn_x, L0_attn_cos, L0_attn_sin, L0_attn_Wq, L0_attn_Wk,
           L0_attn_Wv, L0_attn_Wo, L0_ffn_U, L0_ffn_V, L0_ffn_W1, L0_ffn_b1,
           L0_ffn_W2, L0_ffn_b2):
    x = np.asarray(L0_attn_x, dtype=np.float32)
    Wq = np.asarray(L0_attn_Wq, dtype=np.float32)
    Wk = np.asarray(L0_attn_Wk, dtype=np.float32)
    Wv = np.asarray(L0_attn_Wv, dtype=np.float32)
    Wo = np.asarray(L0_attn_Wo, dtype=np.float32)
    U = np.asarray(L0_ffn_U, dtype=np.float32)
    V = np.asarray(L0_ffn_V, dtype=np.float32)
    W1 = np.asarray(L0_ffn_W1, dtype=np.float32)
    b1 = np.asarray(L0_ffn_b1, dtype=np.float32)
    W2 = np.asarray(L0_ffn_W2, dtype=np.float32)
    b2 = np.asarray(L0_ffn_b2, dtype=np.float32)

    shared = {
        "wqT": np.ascontiguousarray(Wq.T),
        "wkT": np.ascontiguousarray(Wk.T),
        "wvT": np.ascontiguousarray(Wv.T),
        "woT": np.ascontiguousarray(Wo.T),
        "w1": np.ascontiguousarray(W1),
        "w2c": np.ascontiguousarray(W2.reshape(16, 128, 8).transpose(1, 0, 2)),
        "b1c": np.ascontiguousarray(b1.reshape(16, 128).T),
        "b2c": np.ascontiguousarray(b2.reshape(8, 1)),
        "vallT": np.ascontiguousarray(V.transpose(2, 0, 1).reshape(C, RTOT)),
        "uallT": np.ascontiguousarray(U.transpose(0, 2, 1).reshape(RTOT, C)),
    }
    in_maps = []
    for core in range(8):
        b, half = core // 2, core % 2
        xc = np.ascontiguousarray(x[b, half * TOK:(half + 1) * TOK, :].T)
        m = dict(shared)
        m["xT"] = xc
        in_maps.append(m)

    if not _NC_CACHE:
        _NC_CACHE.append(build_program())
    nc = _NC_CACHE[0]
    res = run_bass_kernel_spmd(nc, in_maps, core_ids=list(range(8)))
    out = np.empty((B, T, C), dtype=np.float32)
    for core in range(8):
        b, half = core // 2, core % 2
        out[b, half * TOK:(half + 1) * TOK, :] = res.results[core]["outT"].T
    return out


# version 08
# speedup vs baseline: 1.0312x; 1.0312x over previous
import numpy as np
from contextlib import ExitStack

import concourse.bass as bass
import concourse.mybir as mybir
from concourse.bass_utils import run_bass_kernel_spmd

F32 = mybir.dt.float32
BF16 = mybir.dt.bfloat16
AF = mybir.ActivationFunctionType
ALU = mybir.AluOpType

B, T, C, H, DH = 4, 8192, 2048, 16, 128
TOK = 4096
NCH = 16
T1, T2, T3 = 128, 512, 256
N1, N2, N3 = TOK // T1, TOK // T2, TOK // T3
KEXP, R = 8, 64
RTOT = KEXP * R


def build_program():
    nc = bass.Bass("TRN2", target_bir_lowering=False)

    xT = nc.dram_tensor("xT", [C, TOK], F32, kind="ExternalInput")
    wqT = nc.dram_tensor("wqT", [C, C], F32, kind="ExternalInput")
    wkT = nc.dram_tensor("wkT", [C, C], F32, kind="ExternalInput")
    wvT = nc.dram_tensor("wvT", [C, C], F32, kind="ExternalInput")
    woT = nc.dram_tensor("woT", [C, C], F32, kind="ExternalInput")
    w1 = nc.dram_tensor("w1", [C, C], F32, kind="ExternalInput")
    w2c = nc.dram_tensor("w2c", [128, NCH, KEXP], F32, kind="ExternalInput")
    b1c_d = nc.dram_tensor("b1c", [128, NCH], F32, kind="ExternalInput")
    b2c_d = nc.dram_tensor("b2c", [KEXP, 1], F32, kind="ExternalInput")
    vallT = nc.dram_tensor("vallT", [C, RTOT], F32, kind="ExternalInput")
    uallT = nc.dram_tensor("uallT", [RTOT, C], F32, kind="ExternalInput")
    outT = nc.dram_tensor("outT", [C, TOK], F32, kind="ExternalOutput")
    y1raw_d = nc.dram_tensor("y1raw_d", [C, TOK], F32)
    mstd_d = nc.dram_tensor("mstd_d", [N2, 2 * T2], F32)
    ccin = nc.dram_tensor("ccin", [128, 2064], F32)
    ccout = nc.dram_tensor("ccout", [128, 2064], F32)

    st = ExitStack()
    sem = {}
    for name in [
        "ic1", "gI",
        "wka", "wkb", "wva", "wvb", "x0", "x1",
        "kg1", "vg1", "sc1", "ev1", "kvf", "pe1", "kse", "ccs",
        "ccd", "ccr", "dl2", "wqa", "wqb", "woa", "wob",
        "xb2", "qd2", "sc2", "qe2", "dend2", "zd2", "num2", "numc2",
        "zbm2", "mx2", "attn2", "xfa", "xfb", "y12", "y1sa", "y1sb",
        "cst2", "stt2", "msv2", "rs2", "msd2",
        "y3a", "y3b", "mra", "mrb", "hl3", "bc3", "nr3", "yc3",
        "hg3", "hr3", "lg3", "sm3", "se3", "ds3", "rc3", "rb3",
        "al3", "eg3", "vx3", "vc3", "vs3", "ug3", "st3", "ue3",
        "c23", "mv3", "r23", "b23", "n23", "o3a", "o3b", "o3c", "o3d",
        "w1a", "w1b", "wvu", "vc1", "zbc2", "ec3", "rcb3",
    ]:
        sem[name] = st.enter_context(nc.semaphore(name))
    S = type("S", (), sem)

    ones_col = st.enter_context(nc.sbuf_tensor("ones_col", [128, 1], BF16))
    ones_row = st.enter_context(nc.sbuf_tensor("ones_row", [1, 128], BF16))
    ones8 = st.enter_context(nc.sbuf_tensor("ones8", [8, 1], BF16))
    sel_bf = st.enter_context(nc.sbuf_tensor("sel_bf", [128, 2048], BF16))
    m1sel = st.enter_context(nc.sbuf_tensor("m1sel", [128, 512], BF16))
    b1c = st.enter_context(nc.sbuf_tensor("b1c_s", [128, NCH], F32))
    b2c = st.enter_context(nc.sbuf_tensor("b2c_s", [8, 1], F32))
    w2s = st.enter_context(nc.sbuf_tensor("w2s", [128, NCH, KEXP], BF16))
    dl_bf = st.enter_context(nc.sbuf_tensor("dl_bf", [128, 256], BF16))
    kv_bf = st.enter_context(nc.sbuf_tensor("kv_bf", [128, 2048], BF16))
    ksumf = st.enter_context(nc.sbuf_tensor("ksumf", [128, 16], F32))
    cpool = st.enter_context(nc.sbuf_tensor("cpool", [128, 2], F32))
    nc.const_aps.register(F32, 1e-5, cpool[:, 0:1])
    nc.const_aps.register(F32, -0.5, cpool[:, 1:2])

    if True:
        sti = ExitStack()
        selv1 = sti.enter_context(nc.sbuf_tensor("selv1", [128, 2048], F32))
        selv2 = sti.enter_context(nc.sbuf_tensor("selv2", [128, 2048], F32))
        m1v1 = sti.enter_context(nc.sbuf_tensor("m1v1", [128, 512], F32))
        m1v2 = sti.enter_context(nc.sbuf_tensor("m1v2", [128, 512], F32))
        with nc.Block("init") as blk:
            @blk.gpsimd
            def _(g):
                g.iota(selv1[:, :], [[1, 16], [0, 128]], channel_multiplier=0,
                       allow_small_or_imprecise_dtypes=True)
                g.iota(selv2[:, :], [[0, 2048]], channel_multiplier=1,
                       allow_small_or_imprecise_dtypes=True)
                g.iota(m1v1[:, :], [[1, 8], [0, 64]], channel_multiplier=0,
                       allow_small_or_imprecise_dtypes=True)
                g.iota(m1v2[:, :], [[0, 512]], channel_multiplier=1,
                       allow_small_or_imprecise_dtypes=True)
                g.memset(ones_col[:, :], 1.0)
                g.memset(ones_row[:, :], 1.0)
                g.memset(ones8[:, :], 1.0)
                g.memset(dl_bf[:, :], 0.0)
                g.memset(cpool[:, 0:1], 1e-5)
                g.memset(cpool[:, 1:2], -0.5).then_inc(S.gI, 1)
                g.dma_start(out=b1c[:, :], in_=b1c_d.ap()).then_inc(S.ic1, 16)
                g.dma_start(out=b2c[:, :], in_=b2c_d.ap()).then_inc(S.ic1, 16)
                g.dma_start(out=w2s[:, :, :], in_=w2c.ap()).then_inc(S.ic1, 16)

            @blk.vector
            def _(v):
                v.wait_ge(S.gI, 1)
                v.tensor_tensor(sel_bf[:, :], selv1[:, :], selv2[:, :], ALU.is_equal)
                v.tensor_tensor(m1sel[:, :], m1v1[:, :], m1v2[:, :], ALU.is_equal)

        sti.close()

        with ExitStack() as st1:
            wk_sb = st1.enter_context(nc.sbuf_tensor("wk_sb", [128, NCH, 2048], BF16))
            wv_sb = st1.enter_context(nc.sbuf_tensor("wv_sb", [128, NCH, 2048], BF16))
            xs_sb = st1.enter_context(nc.sbuf_tensor("xs_sb", [128, 2, NCH, T1], BF16))
            k_sb = st1.enter_context(nc.sbuf_tensor("k_sb", [128, 2048], BF16))
            v_sb = st1.enter_context(nc.sbuf_tensor("v_sb", [128, 2048], BF16))
            e1_sb = st1.enter_context(nc.sbuf_tensor("e1_sb", [128, 512], F32))
            r1_sb = st1.enter_context(nc.sbuf_tensor("r1_sb", [128, 512], F32))
            kvpack = st1.enter_context(nc.sbuf_tensor("kvpack", [128, 2064], F32))
            k_ps = st1.enter_context(nc.psum_tensor("k_ps", [128, 512], F32))
            v_ps = st1.enter_context(nc.psum_tensor("v_ps", [128, 512], F32))
            kv_ps = st1.enter_context(nc.psum_tensor("kv_ps", [128, 2048], F32))
            ksum_ps = st1.enter_context(nc.psum_tensor("ksum_ps", [128, 16], F32))

            xTr = xT.ap().rearrange("(cc p) t -> p cc t", p=128)
            wkr = wkT.ap().rearrange("(cc p) o -> p cc o", p=128)
            wvr = wvT.ap().rearrange("(cc p) o -> p cc o", p=128)

            with nc.Block("sweep1") as blk:
                @blk.gpsimd
                def _(g):
                    xsem = [S.x0, S.x1]
                    g.dma_start(out=xs_sb[:, 0, :, :], in_=xTr[:, :, 0:T1]).then_inc(S.x0, 16)
                    g.dma_start(out=xs_sb[:, 1, :, :], in_=xTr[:, :, T1:2 * T1]).then_inc(S.x1, 16)
                    wsem = {("k", 0): S.wka, ("k", 1): S.wkb, ("v", 0): S.wva, ("v", 1): S.wvb}
                    for ob in range(4):
                        for wname, wr, wsb in (("k", wkr, wk_sb), ("v", wvr, wv_sb)):
                            sm_ = wsem[(wname, ob % 2)]
                            g.dma_start(out=wsb[:, :, ob * 512:(ob + 1) * 512],
                                        in_=wr[:, :, ob * 512:(ob + 1) * 512]).then_inc(sm_, 16)
                    for t in range(2, N1):
                        g.wait_ge(S.pe1, t - 1)
                        g.dma_start(out=xs_sb[:, t % 2, :, :],
                                    in_=xTr[:, :, t * T1:(t + 1) * T1]).then_inc(xsem[t % 2], 16)
                    g.wait_ge(S.kvf, N1)
                    g.wait_ge(S.kse, 1)
                    g.dma_start(out=ccin.ap(), in_=kvpack[:, :]).then_inc(S.ccs, 16)

                @blk.tensor
                def _(tt):
                    xsem = [S.x0, S.x1]
                    wsem = {("k", 0): S.wka, ("k", 1): S.wkb, ("v", 0): S.wva, ("v", 1): S.wvb}
                    for t in range(N1):
                        tt.wait_ge(xsem[t % 2], 16 * (t // 2 + 1))
                        for ob in range(4):
                            gidx = 4 * t + ob
                            if t == 0:
                                tt.wait_ge(wsem[("k", ob % 2)], 16 * (ob // 2 + 1))
                            if gidx > 0:
                                tt.wait_ge(S.sc1, 2 * gidx)
                            for c in range(NCH):
                                ins = tt.matmul(k_ps[:, :], xs_sb[:, t % 2, c, :],
                                                wk_sb[:, c, ob * 512:(ob + 1) * 512],
                                                start=(c == 0), stop=(c == NCH - 1))
                                if c == NCH - 1:
                                    ins.then_inc(S.kg1, 1)
                            if t == 0:
                                tt.wait_ge(wsem[("v", ob % 2)], 16 * (ob // 2 + 1))
                            if gidx > 0:
                                tt.wait_ge(S.vc1, gidx)
                            for c in range(NCH):
                                ins = tt.matmul(v_ps[:, :], xs_sb[:, t % 2, c, :],
                                                wv_sb[:, c, ob * 512:(ob + 1) * 512],
                                                start=(c == 0), stop=(c == NCH - 1))
                                if c == NCH - 1:
                                    ins.then_inc(S.vg1, 1)
                        tt.wait_ge(S.ev1, 4 * t + 4)
                        tt.wait_ge(S.vc1, 4 * t + 4)
                        if t > 0:
                            tt.wait_ge(S.kvf, t)
                        for h in range(H):
                            tt.matmul(kv_ps[:, h * 128:(h + 1) * 128],
                                      k_sb[:, h * 128:(h + 1) * 128],
                                      v_sb[:, h * 128:(h + 1) * 128],
                                      start=(h % 4 == 0), stop=(h % 4 == 3),
                                      skip_group_check=True)
                            ins = tt.matmul(ksum_ps[:, h:h + 1],
                                            k_sb[:, h * 128:(h + 1) * 128],
                                            ones_col[:, 0:1],
                                            start=(t == 0 and h == 0),
                                            stop=(t == N1 - 1 and h == H - 1),
                                            skip_group_check=True)
                            if h == H - 1:
                                ins.then_inc(S.pe1, 1)

                @blk.scalar
                def _(s):
                    for t in range(N1):
                        for ob in range(4):
                            gidx = 4 * t + ob
                            s.wait_ge(S.kg1, gidx + 1)
                            if gidx > 0:
                                s.wait_ge(S.ev1, gidx)
                            s.activation(e1_sb[:, :], k_ps[:, :], AF.Exp).then_inc(S.sc1, 1)
                            s.activation(r1_sb[:, :], k_ps[:, :], AF.Relu).then_inc(S.sc1, 1)
                            s.wait_ge(S.vg1, gidx + 1)
                            if ob == 0 and t > 0:
                                s.wait_ge(S.pe1, t)
                            s.copy(v_sb[:, ob * 512:(ob + 1) * 512],
                                   v_ps[:, :]).then_inc(S.vc1, 1)

                @blk.vector
                def _(v):
                    v.memset(kvpack[:, :], 0.0)
                    for t in range(N1):
                        for ob in range(4):
                            gidx = 4 * t + ob
                            if ob == 0 and t > 0:
                                v.wait_ge(S.pe1, t)
                            v.wait_ge(S.sc1, 2 * gidx + 2)
                            v.scalar_tensor_tensor(k_sb[:, ob * 512:(ob + 1) * 512],
                                                   e1_sb[:, :], 1.0, r1_sb[:, :],
                                                   ALU.min, ALU.add).then_inc(S.ev1, 1)
                        v.wait_ge(S.pe1, t + 1)
                        v.tensor_tensor(kvpack[:, 0:2048], kvpack[:, 0:2048],
                                        kv_ps[:, :], ALU.add).then_inc(S.kvf, 1)
                    v.tensor_copy(kvpack[:, 2048:2064], ksum_ps[:, :]).then_inc(S.kse, 1)

        with ExitStack() as st2:
            wq_sb = st2.enter_context(nc.sbuf_tensor("wq_sb", [128, NCH, 2048], BF16))
            wo_sb = st2.enter_context(nc.sbuf_tensor("wo_sb", [128, NCH, 2048], BF16))
            xb_sb = st2.enter_context(nc.sbuf_tensor("xb_sb", [128, NCH, T2], BF16))
            q_sb = st2.enter_context(nc.sbuf_tensor("q_sb", [128, NCH, T2], BF16))
            e2_sb = st2.enter_context(nc.sbuf_tensor("e2_sb", [128, T2], F32))
            r2_sb = st2.enter_context(nc.sbuf_tensor("r2_sb", [128, T2], F32))
            num_sb = st2.enter_context(nc.sbuf_tensor("num_sb", [128, 2, T2], BF16))
            zb_sb = st2.enter_context(nc.sbuf_tensor("zb_sb", [128, 2, T2], F32))
            xf_sb = st2.enter_context(nc.sbuf_tensor("xf_sb", [128, 2, T2], F32))
            y1st_sb = st2.enter_context(nc.sbuf_tensor("y1st_sb", [128, 2, T2], F32))
            y1bf_sb = st2.enter_context(nc.sbuf_tensor("y1bf_sb", [128, 2, T2], BF16))
            sqbf_sb = st2.enter_context(nc.sbuf_tensor("sqbf_sb", [128, 2, T2], BF16))
            z0_sb = st2.enter_context(nc.sbuf_tensor("z0_sb", [16, T2], F32))
            zhi_sb = st2.enter_context(nc.sbuf_tensor("zhi_sb", [128, T2], BF16))
            zlo_sb = st2.enter_context(nc.sbuf_tensor("zlo_sb", [128, T2], BF16))
            mr_sb = st2.enter_context(nc.sbuf_tensor("mr_sb", [1, 2 * T2], F32))
            m2_sb = st2.enter_context(nc.sbuf_tensor("m2_sb", [1, T2], F32))
            var_sb = st2.enter_context(nc.sbuf_tensor("var_sb", [1, T2], F32))
            qn_ps = [st2.enter_context(nc.psum_tensor(f"qn_ps{i}", [128, T2], F32)) for i in range(2)]
            dsq_ps = st2.enter_context(nc.psum_tensor("dsq_ps", [16, T2], F32))
            zb_ps = [st2.enter_context(nc.psum_tensor(f"zb_ps{i}", [128, T2], F32)) for i in range(2)]
            at_ps = [st2.enter_context(nc.psum_tensor(f"at_ps{i}", [128, T2], F32)) for i in range(2)]
            ssum_ps = st2.enter_context(nc.psum_tensor("ssum_ps", [1, T2], F32))

            xTr2 = xT.ap().rearrange("(cc p) t -> p cc t", p=128)
            wqr = wqT.ap().rearrange("(cc p) o -> p cc o", p=128)
            wor = woT.ap().rearrange("(cc p) o -> p cc o", p=128)
            wqsem = [S.wqa, S.wqb]
            wosem = [S.woa, S.wob]
            xfsem = [S.xfa, S.xfb]
            y1sem = [S.y1sa, S.y1sb]

            with nc.Block("sweep2") as blk:
                @blk.gpsimd
                def _(g):
                    g.dma_start(out=xb_sb[:, :, :], in_=xTr2[:, :, 0:T2]).then_inc(S.xb2, 16)
                    for h in range(NCH):
                        g.dma_start(out=wq_sb[:, :, h * 128:(h + 1) * 128],
                                    in_=wqr[:, :, h * 128:(h + 1) * 128]).then_inc(wqsem[h % 2], 16)
                    g.wait_ge(S.ccs, 16)
                    g.collective_compute(
                        "AllReduce", ALU.add,
                        replica_groups=[[0, 1], [2, 3], [4, 5], [6, 7]],
                        ins=[ccin.ap().opt()], outs=[ccout.ap().opt()],
                    ).then_inc(S.ccd, 1)
                    g.wait_ge(S.ccd, 1)
                    g.dma_start(out=kv_bf[:, :], in_=ccout.ap()[:, 0:2048]).then_inc(S.ccr, 16)
                    g.dma_start(out=ksumf[:, :], in_=ccout.ap()[:, 2048:2064]).then_inc(S.ccr, 16)
                    for o in range(NCH):
                        g.dma_start(out=wo_sb[:, :, o * 128:(o + 1) * 128],
                                    in_=wor[:, :, o * 128:(o + 1) * 128]).then_inc(wosem[o % 2], 16)
                    for j in range(1, N2):
                        g.wait_ge(S.qd2, 16 * j)
                        g.dma_start(out=xb_sb[:, :, :],
                                    in_=xTr2[:, :, j * T2:(j + 1) * T2]).then_inc(S.xb2, 16)

                @blk.tensor
                def _(tt):
                    for j in range(N2):
                        for h in range(NCH):
                            gq = NCH * j + h
                            if j == 0:
                                tt.wait_ge(wqsem[h % 2], 16 * (h // 2 + 1))
                            if h == 0:
                                tt.wait_ge(S.xb2, 16 * (j + 1))
                                if j > 0:
                                    tt.wait_ge(S.numc2, 16 * j)
                            if gq >= 2:
                                tt.wait_ge(S.sc2, 2 * (gq - 1))
                            for c in range(NCH):
                                ins = tt.matmul(qn_ps[h % 2][:, :],
                                                wq_sb[:, c, h * 128:(h + 1) * 128],
                                                xb_sb[:, c, :],
                                                start=(c == 0), stop=(c == NCH - 1))
                                if c == NCH - 1:
                                    ins.then_inc(S.qd2, 1)
                        if j == 0:
                            tt.wait_ge(S.dl2, 1)
                        tt.wait_ge(S.qe2, NCH * (j + 1))
                        if j > 0:
                            tt.wait_ge(S.zd2, 4 * j)
                            tt.wait_ge(S.msv2, 3 * j)
                        for h in range(NCH):
                            ins = tt.matmul(dsq_ps[0:16, :], dl_bf[:, h * 16:(h + 1) * 16],
                                            q_sb[:, h, :], start=(h == 0), stop=(h == NCH - 1),
                                            skip_group_check=True)
                            if h == NCH - 1:
                                ins.then_inc(S.dend2, 1)
                        for h in range(NCH):
                            gq = NCH * j + h
                            if h == 0:
                                tt.wait_ge(S.zd2, 4 * (j + 1))
                                tt.wait_ge(S.sc2, 32 * (j + 1))
                                if j == 0:
                                    tt.wait_ge(S.ccr, 32)
                            if gq >= 2:
                                tt.wait_ge(S.numc2, gq - 1)
                            tt.matmul(qn_ps[h % 2][:, :], kv_bf[:, h * 128:(h + 1) * 128],
                                      q_sb[:, h, :], start=True, stop=True).then_inc(S.num2, 1)
                            if gq >= 2:
                                tt.wait_ge(S.zbc2, gq - 1)
                            tt.matmul(zb_ps[h % 2][:, :], sel_bf[:, h * 128:(h + 1) * 128],
                                      zhi_sb[:, :], start=True, stop=False,
                                      skip_group_check=True)
                            tt.matmul(zb_ps[h % 2][:, :], sel_bf[:, h * 128:(h + 1) * 128],
                                      zlo_sb[:, :], start=False, stop=True,
                                      skip_group_check=True).then_inc(S.zbm2, 1)
                        def stats(op):
                            tt.wait_ge(S.cst2, 2 * (NCH * j + op + 1))
                            if op == 0:
                                tt.wait_ge(S.zd2, 4 * j + 1)
                                if j > 0:
                                    tt.wait_ge(S.msv2, 3 * j)
                            tt.matmul(ssum_ps[0:1, :], ones_col[:, 0:1],
                                      y1bf_sb[:, op % 2, :], start=(op == 0),
                                      stop=(op == NCH - 1), skip_group_check=True)
                            ins = tt.matmul(dsq_ps[0:1, :], ones_col[:, 0:1],
                                            sqbf_sb[:, op % 2, :], start=(op == 0),
                                            stop=(op == NCH - 1), skip_group_check=True)
                            ins.then_inc(S.stt2, 1)
                        for o in range(NCH):
                            go = NCH * j + o
                            if j == 0:
                                tt.wait_ge(wosem[o % 2], 16 * (o // 2 + 1))
                            if o == 0:
                                tt.wait_ge(S.mx2, NCH * (j + 1))
                            if go >= 2:
                                tt.wait_ge(S.y12, go - 1)
                            for c in range(NCH):
                                ins = tt.matmul(at_ps[o % 2][:, :],
                                                wo_sb[:, c, o * 128:(o + 1) * 128],
                                                q_sb[:, c, :],
                                                start=(c == 0), stop=(c == NCH - 1))
                                if c == NCH - 1:
                                    ins.then_inc(S.attn2, 1)
                            if o >= 1:
                                stats(o - 1)
                        stats(NCH - 1)

                @blk.scalar
                def _(s):
                    for j in range(N2):
                        for h in range(NCH):
                            gq = NCH * j + h
                            s.wait_ge(S.qd2, gq + 1)
                            if gq >= 1:
                                s.wait_ge(S.qe2, gq)
                            s.activation(e2_sb[:, :], qn_ps[h % 2][:, :], AF.Exp).then_inc(S.sc2, 1)
                            s.activation(r2_sb[:, :], qn_ps[h % 2][:, :], AF.Relu).then_inc(S.sc2, 1)
                        for h in range(NCH):
                            gq = NCH * j + h
                            s.wait_ge(S.num2, gq + 1)
                            if gq >= 2:
                                s.wait_ge(S.mx2, gq - 1)
                            s.copy(num_sb[:, h % 2, :], qn_ps[h % 2][:, :]).then_inc(S.numc2, 1)
                            s.wait_ge(S.zbm2, gq + 1)
                            s.copy(zb_sb[:, h % 2, :], zb_ps[h % 2][:, :]).then_inc(S.zbc2, 1)
                        for o in range(NCH):
                            go = NCH * j + o
                            s.wait_ge(S.y12, go + 1)
                            if go >= 2:
                                s.wait_ge(S.stt2, go - 1)
                            s.copy(y1bf_sb[:, o % 2, :], y1st_sb[:, o % 2, :]).then_inc(S.cst2, 1)
                            s.activation(sqbf_sb[:, o % 2, :], y1st_sb[:, o % 2, :],
                                         AF.Square).then_inc(S.cst2, 1)
                        s.wait_ge(S.msv2, 3 * (j + 1))
                        if j > 0:
                            s.wait_ge(S.msd2, 16 * j)
                        s.activation(m2_sb[0:1, :], var_sb[0:1, :], AF.Ln, bias=1e-5)
                        s.activation(mr_sb[0:1, T2:2 * T2], m2_sb[0:1, :], AF.Exp,
                                     scale=-0.5).then_inc(S.rs2, 1)

                @blk.vector
                def _(v):
                    for j in range(N2):
                        for h in range(NCH):
                            gq = NCH * j + h
                            v.wait_ge(S.sc2, 2 * gq + 2)
                            v.scalar_tensor_tensor(q_sb[:, h, :], e2_sb[:, :], 1.0,
                                                   r2_sb[:, :], ALU.min, ALU.add).then_inc(S.qe2, 1)
                        if j == 0:
                            v.wait_ge(S.ccr, 32)
                            for h in range(H):
                                ins = v.tensor_copy(dl_bf[:, h * 16 + h:h * 16 + h + 1],
                                                    ksumf[:, h:h + 1])
                                if h == H - 1:
                                    ins.then_inc(S.dl2, 1)
                        if j == 0:
                            v.memset(zhi_sb[:, :], 0.0)
                            v.memset(zlo_sb[:, :], 0.0)
                        v.wait_ge(S.dend2, j + 1)
                        v.tensor_scalar_add(z0_sb[0:16, :], dsq_ps[0:16, :], 1e-6).then_inc(S.zd2, 1)
                        v.reciprocal(z0_sb[0:16, :], z0_sb[0:16, :]).then_inc(S.zd2, 1)
                        v.tensor_copy(zhi_sb[0:16, :], z0_sb[0:16, :]).then_inc(S.zd2, 1)
                        v.tensor_tensor(zlo_sb[0:16, :], z0_sb[0:16, :], zhi_sb[0:16, :],
                                        ALU.subtract).then_inc(S.zd2, 1)
                        for h in range(NCH):
                            gq = NCH * j + h
                            v.wait_ge(S.zbc2, gq + 1)
                            v.wait_ge(S.numc2, gq + 1)
                            v.tensor_tensor(q_sb[:, h, :], num_sb[:, h % 2, :],
                                            zb_sb[:, h % 2, :], ALU.mult).then_inc(S.mx2, 1)
                        for o in range(NCH):
                            go = NCH * j + o
                            v.wait_ge(S.attn2, go + 1)
                            v.wait_ge(xfsem[o % 2], 16 * (8 * j + o // 2 + 1))
                            if 8 * j + o // 2 >= 1:
                                v.wait_ge(y1sem[o % 2], 16 * (8 * j + o // 2))
                            if go >= 2:
                                v.wait_ge(S.cst2, 2 * (go - 1))
                            v.tensor_tensor(y1st_sb[:, o % 2, :], xf_sb[:, o % 2, :],
                                            at_ps[o % 2][:, :], ALU.add).then_inc(S.y12, 1)
                        v.wait_ge(S.stt2, NCH * (j + 1))
                        if j > 0:
                            v.wait_ge(S.msd2, 16 * j)
                        v.tensor_scalar_mul(mr_sb[0:1, 0:T2], ssum_ps[0:1, :],
                                            1.0 / 2048.0).then_inc(S.msv2, 1)
                        v.tensor_tensor(m2_sb[0:1, :], mr_sb[0:1, 0:T2], mr_sb[0:1, 0:T2],
                                        ALU.mult).then_inc(S.msv2, 1)
                        v.scalar_tensor_tensor(var_sb[0:1, :], dsq_ps[0:1, :], 1.0 / 2048.0,
                                               m2_sb[0:1, :], ALU.mult,
                                               ALU.subtract).then_inc(S.msv2, 1)

                @blk.sync
                def _(sy):
                    sy.wait_ge(S.ccs, 16)
                    for j in range(N2):
                        def xf_load(o):
                            go = NCH * j + o
                            if go >= 2:
                                sy.wait_ge(S.y12, go - 1)
                            sy.dma_start(out=xf_sb[:, o % 2, :],
                                         in_=xT.ap()[o * 128:(o + 1) * 128,
                                                     j * T2:(j + 1) * T2]).then_inc(xfsem[o % 2], 16)
                        xf_load(0)
                        xf_load(1)
                        for o in range(NCH):
                            sy.wait_ge(S.y12, NCH * j + o + 1)
                            sy.dma_start(out=y1raw_d.ap()[o * 128:(o + 1) * 128,
                                                          j * T2:(j + 1) * T2],
                                         in_=y1st_sb[:, o % 2, :]).then_inc(y1sem[o % 2], 16)
                            if o + 2 < NCH:
                                xf_load(o + 2)
                        sy.wait_ge(S.rs2, j + 1)
                        sy.dma_start(out=mstd_d.ap()[j:j + 1, :],
                                     in_=mr_sb[0:1, :]).then_inc(S.msd2, 16)
                    sy.wait_ge(S.y1sa, 16 * 8 * N2)
                    sy.wait_ge(S.y1sb, 16 * 8 * N2)
                    sy.wait_ge(S.msd2, 16 * N2)

        with ExitStack() as st3:
            w1_sb = st3.enter_context(nc.sbuf_tensor("w1_sb", [128, NCH, 2048], BF16))
            vall_sb = st3.enter_context(nc.sbuf_tensor("vall_sb", [128, NCH, RTOT], BF16))
            uall_sb = st3.enter_context(nc.sbuf_tensor("uall_sb", [128, 4, 2048], BF16))
            y1f_sb = st3.enter_context(nc.sbuf_tensor("y1f_sb", [128, 2, NCH, T3], F32))
            y1b3_sb = st3.enter_context(nc.sbuf_tensor("y1b3_sb", [128, NCH, T3], BF16))
            h_sb = st3.enter_context(nc.sbuf_tensor("h_sb", [128, NCH, T3], BF16))
            vxbf_sb = st3.enter_context(nc.sbuf_tensor("vxbf_sb", [128, 2, T3], BF16))
            ecp_sb = st3.enter_context(nc.sbuf_tensor("ecp_sb", [128, 2, T3], F32))
            rcb_sb = st3.enter_context(nc.sbuf_tensor("rcb_sb", [8, T3], F32))
            vxs_sb = st3.enter_context(nc.sbuf_tensor("vxs_sb", [128, 4, T3], BF16))
            y2raw_sb = st3.enter_context(nc.sbuf_tensor("y2raw_sb", [128, NCH, T3], F32))
            outst_sb = st3.enter_context(nc.sbuf_tensor("outst_sb", [128, 4, T3], F32))
            y2bf_sb = st3.enter_context(nc.sbuf_tensor("y2bf_sb", [128, 2, T3], BF16))
            sq2_sb = st3.enter_context(nc.sbuf_tensor("sq2_sb", [128, 2, T3], BF16))
            mr1_sb = st3.enter_context(nc.sbuf_tensor("mr1_sb", [1, 2, 512], F32))
            m1hi = st3.enter_context(nc.sbuf_tensor("m1hi", [1, T3], BF16))
            m1lo = st3.enter_context(nc.sbuf_tensor("m1lo", [1, T3], BF16))
            r1hi = st3.enter_context(nc.sbuf_tensor("r1hi", [1, T3], BF16))
            r1lo = st3.enter_context(nc.sbuf_tensor("r1lo", [1, T3], BF16))
            lg_sb = st3.enter_context(nc.sbuf_tensor("lg_sb", [8, T3], F32))
            expE_sb = st3.enter_context(nc.sbuf_tensor("expE_sb", [8, T3], BF16))
            alpha_sb = st3.enter_context(nc.sbuf_tensor("alpha_sb", [128, T3], BF16))
            rcp_sb = st3.enter_context(nc.sbuf_tensor("rcp_sb", [1, T3], F32))
            rcphi = st3.enter_context(nc.sbuf_tensor("rcphi", [1, T3], BF16))
            rcplo = st3.enter_context(nc.sbuf_tensor("rcplo", [1, T3], BF16))
            m2m_sb = st3.enter_context(nc.sbuf_tensor("m2m_sb", [1, T3], F32))
            m2sq_sb = st3.enter_context(nc.sbuf_tensor("m2sq_sb", [1, T3], F32))
            var2_sb = st3.enter_context(nc.sbuf_tensor("var2_sb", [1, T3], F32))
            lnv2_sb = st3.enter_context(nc.sbuf_tensor("lnv2_sb", [1, T3], F32))
            rstd2_sb = st3.enter_context(nc.sbuf_tensor("rstd2_sb", [1, T3], F32))
            m2hi = st3.enter_context(nc.sbuf_tensor("m2hi", [1, T3], BF16))
            m2lo = st3.enter_context(nc.sbuf_tensor("m2lo", [1, T3], BF16))
            r2hi = st3.enter_context(nc.sbuf_tensor("r2hi", [1, T3], BF16))
            r2lo = st3.enter_context(nc.sbuf_tensor("r2lo", [1, T3], BF16))
            hv_ps = [st3.enter_context(nc.psum_tensor(f"hv_ps{i}", [128, T3], F32)) for i in range(2)]
            lgd_ps = st3.enter_context(nc.psum_tensor("lgd_ps", [8, T3], F32))
            rcpb_ps = st3.enter_context(nc.psum_tensor("rcpb_ps", [8, T3], F32))
            e_ps = [st3.enter_context(nc.psum_tensor(f"e_ps{i}", [128, T3], F32)) for i in range(2)]
            m1b_ps = st3.enter_context(nc.psum_tensor("m1b_ps", [128, T3], F32))
            r1b_ps = st3.enter_context(nc.psum_tensor("r1b_ps", [128, T3], F32))

            y1r = y1raw_d.ap().rearrange("(cc p) t -> p cc t", p=128)
            w1r = w1.ap().rearrange("(dc p) e -> p dc e", p=128)
            vr = vallT.ap().rearrange("(dc p) r -> p dc r", p=128)
            ur = uallT.ap().rearrange("(rc p) d -> p rc d", p=128)
            y3sem = [S.y3a, S.y3b]
            mrsem = [S.mra, S.mrb]
            w1sem = [S.w1a, S.w1b]
            o3sem = [S.o3a, S.o3b, S.o3c, S.o3d]

            with nc.Block("sweep3") as blk:
                @blk.gpsimd
                def _(g):
                    for oc in range(NCH):
                        g.wait_ge(S.qd2, 113 + oc)
                        g.dma_start(out=w1_sb[:, :, oc * 128:(oc + 1) * 128],
                                    in_=w1r[:, :, oc * 128:(oc + 1) * 128]).then_inc(w1sem[oc % 2], 16)
                    g.wait_ge(S.attn2, 128)
                    g.dma_start(out=vall_sb[:, :, :], in_=vr[:, :, :]).then_inc(S.wvu, 16)
                    g.dma_start(out=uall_sb[:, :, :], in_=ur[:, :, :]).then_inc(S.wvu, 16)

                @blk.tensor
                def _(tt):
                    for b in range(N3):
                        tt.wait_ge(S.hl3, 4 * (b + 1))
                        if b > 0:
                            tt.wait_ge(S.n23, 32 * b)
                        tt.matmul(m1b_ps[:, :], ones_row[0:1, :], m1hi[0:1, :],
                                  start=True, stop=False, skip_group_check=True)
                        tt.matmul(m1b_ps[:, :], ones_row[0:1, :], m1lo[0:1, :],
                                  start=False, stop=True, skip_group_check=True).then_inc(S.bc3, 1)
                        tt.matmul(r1b_ps[:, :], ones_row[0:1, :], r1hi[0:1, :],
                                  start=True, stop=False, skip_group_check=True)
                        tt.matmul(r1b_ps[:, :], ones_row[0:1, :], r1lo[0:1, :],
                                  start=False, stop=True, skip_group_check=True).then_inc(S.bc3, 1)
                        for oc in range(NCH):
                            g3 = NCH * b + oc
                            if b == 0:
                                tt.wait_ge(w1sem[oc % 2], 16 * (oc // 2 + 1))
                            if oc == 0:
                                tt.wait_ge(S.yc3, NCH * (b + 1))
                            if g3 >= 2:
                                tt.wait_ge(S.hr3, g3 - 1)
                            for dc in range(NCH):
                                ins = tt.matmul(hv_ps[oc % 2][:, :],
                                                w1_sb[:, dc, oc * 128:(oc + 1) * 128],
                                                y1b3_sb[:, dc, :],
                                                start=(dc == 0), stop=(dc == NCH - 1))
                                if dc == NCH - 1:
                                    ins.then_inc(S.hg3, 1)
                        tt.wait_ge(S.hr3, NCH * (b + 1))
                        if b == 0:
                            tt.wait_ge(S.ic1, 48)
                        else:
                            tt.wait_ge(S.mv3, 7 * b - 4)
                        for ec in range(NCH):
                            ins = tt.matmul(lgd_ps[0:8, :], w2s[:, ec, :], h_sb[:, ec, :],
                                            start=(ec == 0), stop=(ec == NCH - 1),
                                            skip_group_check=True)
                            if ec == NCH - 1:
                                ins.then_inc(S.lg3, 1)
                        tt.wait_ge(S.se3, b + 1)
                        tt.wait_ge(S.sm3, 2 * (b + 1))
                        tt.matmul(lgd_ps[0:1, :], ones8[0:8, 0:1], expE_sb[0:8, :],
                                  start=True, stop=True, skip_group_check=True).then_inc(S.ds3, 1)
                        tt.wait_ge(S.rc3, 3 * (b + 1))
                        if b > 0:
                            tt.wait_ge(S.rcb3, b)
                        tt.matmul(rcpb_ps[0:8, :], ones_row[0:1, 0:8], rcphi[0:1, :],
                                  start=True, stop=False, skip_group_check=True)
                        tt.matmul(rcpb_ps[0:8, :], ones_row[0:1, 0:8], rcplo[0:1, :],
                                  start=False, stop=True, skip_group_check=True).then_inc(S.rb3, 1)
                        for rc in range(4):
                            g4 = 4 * b + rc
                            if rc == 0:
                                tt.wait_ge(S.al3, b + 1)
                                if b == 0:
                                    tt.wait_ge(S.wvu, 32)
                            if g4 >= 2:
                                tt.wait_ge(S.ec3, g4 - 1)
                            tt.matmul(e_ps[rc % 2][:, :], m1sel[:, rc * 128:(rc + 1) * 128],
                                      alpha_sb[:, :], start=True, stop=True).then_inc(S.eg3, 1)
                            if g4 >= 2:
                                tt.wait_ge(S.vc3, g4 - 1)
                            for dc in range(NCH):
                                ins = tt.matmul(hv_ps[rc % 2][:, :],
                                                vall_sb[:, dc, rc * 128:(rc + 1) * 128],
                                                y1b3_sb[:, dc, :],
                                                start=(dc == 0), stop=(dc == NCH - 1))
                                if dc == NCH - 1:
                                    ins.then_inc(S.vx3, 1)
                        def stats2(op):
                            tt.wait_ge(S.c23, 2 * (NCH * b + op + 1))
                            tt.matmul(rcpb_ps[0:1, :], ones_col[:, 0:1],
                                      y2bf_sb[:, op % 2, :], start=(op == 0),
                                      stop=(op == NCH - 1), skip_group_check=True)
                            ins = tt.matmul(lgd_ps[0:1, :], ones_col[:, 0:1],
                                            sq2_sb[:, op % 2, :], start=(op == 0),
                                            stop=(op == NCH - 1), skip_group_check=True)
                            ins.then_inc(S.st3, 1)
                        for oc in range(NCH):
                            g3 = NCH * b + oc
                            if oc == 0:
                                tt.wait_ge(S.vs3, 4 * (b + 1))
                                tt.wait_ge(S.vc3, 4 * (b + 1))
                            if g3 >= 2:
                                tt.wait_ge(S.ue3, g3 - 1)
                            for rc in range(4):
                                ins = tt.matmul(hv_ps[oc % 2][:, :],
                                                uall_sb[:, rc, oc * 128:(oc + 1) * 128],
                                                vxs_sb[:, rc, :],
                                                start=(rc == 0), stop=(rc == 3))
                                if rc == 3:
                                    ins.then_inc(S.ug3, 1)
                            if oc >= 1:
                                stats2(oc - 1)
                        stats2(NCH - 1)
                        tt.wait_ge(S.mv3, 7 * (b + 1))
                        tt.wait_ge(S.nr3, 32 * (b + 1))
                        tt.matmul(m1b_ps[:, :], ones_row[0:1, :], m2hi[0:1, :],
                                  start=True, stop=False, skip_group_check=True)
                        tt.matmul(m1b_ps[:, :], ones_row[0:1, :], m2lo[0:1, :],
                                  start=False, stop=True, skip_group_check=True).then_inc(S.b23, 1)
                        tt.matmul(r1b_ps[:, :], ones_row[0:1, :], r2hi[0:1, :],
                                  start=True, stop=False, skip_group_check=True)
                        tt.matmul(r1b_ps[:, :], ones_row[0:1, :], r2lo[0:1, :],
                                  start=False, stop=True, skip_group_check=True).then_inc(S.b23, 1)

                @blk.scalar
                def _(s):
                    for b in range(N3):
                        for cc in range(NCH):
                            s.wait_ge(S.nr3, 32 * b + 2 * (cc + 1))
                            if cc == 0 and b > 0:
                                s.wait_ge(S.vx3, 4 * b)
                            s.copy(y1b3_sb[:, cc, :], y1f_sb[:, b % 2, cc, :]).then_inc(S.yc3, 1)
                        for oc in range(NCH):
                            g3 = NCH * b + oc
                            s.wait_ge(S.hg3, g3 + 1)
                            if b == 0 and oc == 0:
                                s.wait_ge(S.ic1, 48)
                            s.activation(h_sb[:, oc, :], hv_ps[oc % 2][:, :], AF.Relu,
                                         bias=b1c[:, oc:oc + 1]).then_inc(S.hr3, 1)
                        s.wait_ge(S.sm3, 2 * (b + 1))
                        s.activation(expE_sb[0:8, :], lg_sb[0:8, :], AF.Exp).then_inc(S.se3, 1)
                        s.wait_ge(S.rb3, b + 1)
                        if b > 0:
                            s.wait_ge(S.al3, b)
                        s.copy(rcb_sb[0:8, :], rcpb_ps[0:8, :]).then_inc(S.rcb3, 1)
                        for rc in range(4):
                            g4 = 4 * b + rc
                            s.wait_ge(S.vx3, g4 + 1)
                            if g4 >= 2:
                                s.wait_ge(S.vs3, g4 - 1)
                            s.copy(vxbf_sb[:, rc % 2, :], hv_ps[rc % 2][:, :]).then_inc(S.vc3, 1)
                            s.wait_ge(S.eg3, g4 + 1)
                            s.copy(ecp_sb[:, rc % 2, :], e_ps[rc % 2][:, :]).then_inc(S.ec3, 1)
                        for oc in range(NCH):
                            g3 = NCH * b + oc
                            s.wait_ge(S.ue3, g3 + 1)
                            if g3 >= 2:
                                s.wait_ge(S.st3, g3 - 1)
                            s.copy(y2bf_sb[:, oc % 2, :], y2raw_sb[:, oc, :]).then_inc(S.c23, 1)
                            s.activation(sq2_sb[:, oc % 2, :], y2raw_sb[:, oc, :],
                                         AF.Square).then_inc(S.c23, 1)
                        s.wait_ge(S.mv3, 7 * b + 3)
                        s.activation(lnv2_sb[0:1, :], var2_sb[0:1, :], AF.Ln, bias=1e-5)
                        s.activation(rstd2_sb[0:1, :], lnv2_sb[0:1, :], AF.Exp,
                                     scale=-0.5).then_inc(S.r23, 1)

                @blk.vector
                def _(v):
                    for b in range(N3):
                        v.wait_ge(mrsem[b % 2], 32 * (b // 2 + 1))
                        if b > 0:
                            v.wait_ge(S.bc3, 2 * b)
                        v.tensor_copy(m1hi[0:1, :], mr1_sb[0:1, b % 2, 0:T3]).then_inc(S.hl3, 1)
                        v.tensor_tensor(m1lo[0:1, :], mr1_sb[0:1, b % 2, 0:T3], m1hi[0:1, :],
                                        ALU.subtract).then_inc(S.hl3, 1)
                        v.tensor_copy(r1hi[0:1, :], mr1_sb[0:1, b % 2, 256:256 + T3]).then_inc(S.hl3, 1)
                        v.tensor_tensor(r1lo[0:1, :], mr1_sb[0:1, b % 2, 256:256 + T3],
                                        r1hi[0:1, :], ALU.subtract).then_inc(S.hl3, 1)
                        for cc in range(NCH):
                            if cc == 0:
                                v.wait_ge(S.bc3, 2 * (b + 1))
                                v.wait_ge(y3sem[b % 2], 16 * (b // 2 + 1))
                            v.tensor_tensor(y1f_sb[:, b % 2, cc, :], y1f_sb[:, b % 2, cc, :],
                                            m1b_ps[:, :], ALU.subtract).then_inc(S.nr3, 1)
                            v.tensor_tensor(y1f_sb[:, b % 2, cc, :], y1f_sb[:, b % 2, cc, :],
                                            r1b_ps[:, :], ALU.mult).then_inc(S.nr3, 1)
                        if b == 0:
                            v.memset(alpha_sb[:, :], 0.0)
                        v.wait_ge(S.lg3, b + 1)
                        v.tensor_scalar(lg_sb[0:8, :], lgd_ps[0:8, :], b2c[0:8, 0:1], 50.0,
                                        ALU.add, ALU.min).then_inc(S.sm3, 1)
                        v.tensor_scalar_max(lg_sb[0:8, :], lg_sb[0:8, :], -50.0).then_inc(S.sm3, 1)
                        v.wait_ge(S.ds3, b + 1)
                        v.reciprocal(rcp_sb[0:1, :], lgd_ps[0:1, :]).then_inc(S.rc3, 1)
                        v.tensor_copy(rcphi[0:1, :], rcp_sb[0:1, :]).then_inc(S.rc3, 1)
                        v.tensor_tensor(rcplo[0:1, :], rcp_sb[0:1, :], rcphi[0:1, :],
                                        ALU.subtract).then_inc(S.rc3, 1)
                        v.wait_ge(S.rcb3, b + 1)
                        v.wait_ge(S.se3, b + 1)
                        v.tensor_tensor(alpha_sb[0:8, :], expE_sb[0:8, :], rcb_sb[0:8, :],
                                        ALU.mult).then_inc(S.al3, 1)
                        for rc in range(4):
                            g4 = 4 * b + rc
                            v.wait_ge(S.vc3, g4 + 1)
                            v.wait_ge(S.ec3, g4 + 1)
                            v.tensor_tensor(vxs_sb[:, rc, :], vxbf_sb[:, rc % 2, :],
                                            ecp_sb[:, rc % 2, :], ALU.mult).then_inc(S.vs3, 1)
                        for oc in range(NCH):
                            g3 = NCH * b + oc
                            v.wait_ge(S.ug3, g3 + 1)
                            v.tensor_tensor(y2raw_sb[:, oc, :], y1f_sb[:, b % 2, oc, :],
                                            hv_ps[oc % 2][:, :], ALU.add).then_inc(S.ue3, 1)
                        v.wait_ge(S.st3, NCH * (b + 1))
                        v.tensor_scalar_mul(m2m_sb[0:1, :], rcpb_ps[0:1, :],
                                            1.0 / 2048.0).then_inc(S.mv3, 1)
                        v.tensor_tensor(m2sq_sb[0:1, :], m2m_sb[0:1, :], m2m_sb[0:1, :],
                                        ALU.mult).then_inc(S.mv3, 1)
                        v.scalar_tensor_tensor(var2_sb[0:1, :], lgd_ps[0:1, :], 1.0 / 2048.0,
                                               m2sq_sb[0:1, :], ALU.mult,
                                               ALU.subtract).then_inc(S.mv3, 1)
                        v.wait_ge(S.r23, b + 1)
                        if b > 0:
                            v.wait_ge(S.b23, 2 * b)
                        v.tensor_copy(m2hi[0:1, :], m2m_sb[0:1, :]).then_inc(S.mv3, 1)
                        v.tensor_tensor(m2lo[0:1, :], m2m_sb[0:1, :], m2hi[0:1, :],
                                        ALU.subtract).then_inc(S.mv3, 1)
                        v.tensor_copy(r2hi[0:1, :], rstd2_sb[0:1, :]).then_inc(S.mv3, 1)
                        v.tensor_tensor(r2lo[0:1, :], rstd2_sb[0:1, :], r2hi[0:1, :],
                                        ALU.subtract).then_inc(S.mv3, 1)
                        for oc in range(NCH):
                            if oc == 0:
                                v.wait_ge(S.b23, 2 * (b + 1))
                            v.tensor_tensor(y2raw_sb[:, oc, :], y2raw_sb[:, oc, :],
                                            m1b_ps[:, :], ALU.subtract).then_inc(S.n23, 1)
                            if 4 * b + oc // 4 >= 1:
                                v.wait_ge(o3sem[oc % 4], 16 * (4 * b + oc // 4))
                            v.tensor_tensor(outst_sb[:, oc % 4, :], y2raw_sb[:, oc, :],
                                            r1b_ps[:, :], ALU.mult).then_inc(S.n23, 1)

                @blk.sync
                def _(sy):
                    def y1_load(b):
                        sy.dma_start(out=y1f_sb[:, b % 2, :, :],
                                     in_=y1r[:, :, b * T3:(b + 1) * T3]).then_inc(y3sem[b % 2], 16)

                    def mr_load(b):
                        j2, hf = b // 2, b % 2
                        sy.dma_start(out=mr1_sb[0:1, b % 2, 0:T3],
                                     in_=mstd_d.ap()[j2:j2 + 1,
                                                     hf * T3:hf * T3 + T3]).then_inc(mrsem[b % 2], 16)
                        sy.dma_start(out=mr1_sb[0:1, b % 2, 256:256 + T3],
                                     in_=mstd_d.ap()[j2:j2 + 1,
                                                     T2 + hf * T3:T2 + hf * T3 + T3]).then_inc(mrsem[b % 2], 16)
                    y1_load(0)
                    mr_load(0)
                    y1_load(1)
                    mr_load(1)
                    for b in range(N3):
                        for oc in range(NCH):
                            sy.wait_ge(S.n23, 32 * b + 2 * (oc + 1))
                            sy.dma_start(out=outT.ap()[oc * 128:(oc + 1) * 128,
                                                       b * T3:(b + 1) * T3],
                                         in_=outst_sb[:, oc % 4, :]).then_inc(o3sem[oc % 4], 16)
                        if b + 2 < N3:
                            sy.wait_ge(S.ue3, 16 * (b + 1))
                            y1_load(b + 2)
                            sy.wait_ge(S.hl3, 4 * (b + 1))
                            mr_load(b + 2)
                    for s4 in o3sem:
                        sy.wait_ge(s4, 16 * 4 * N3)

    st.close()
    return nc


_NC_CACHE = []


def kernel(L0_attn_x, L0_attn_cos, L0_attn_sin, L0_attn_Wq, L0_attn_Wk,
           L0_attn_Wv, L0_attn_Wo, L0_ffn_U, L0_ffn_V, L0_ffn_W1, L0_ffn_b1,
           L0_ffn_W2, L0_ffn_b2):
    x = np.asarray(L0_attn_x, dtype=np.float32)
    Wq = np.asarray(L0_attn_Wq, dtype=np.float32)
    Wk = np.asarray(L0_attn_Wk, dtype=np.float32)
    Wv = np.asarray(L0_attn_Wv, dtype=np.float32)
    Wo = np.asarray(L0_attn_Wo, dtype=np.float32)
    U = np.asarray(L0_ffn_U, dtype=np.float32)
    V = np.asarray(L0_ffn_V, dtype=np.float32)
    W1 = np.asarray(L0_ffn_W1, dtype=np.float32)
    b1 = np.asarray(L0_ffn_b1, dtype=np.float32)
    W2 = np.asarray(L0_ffn_W2, dtype=np.float32)
    b2 = np.asarray(L0_ffn_b2, dtype=np.float32)

    shared = {
        "wqT": np.ascontiguousarray(Wq.T),
        "wkT": np.ascontiguousarray(Wk.T),
        "wvT": np.ascontiguousarray(Wv.T),
        "woT": np.ascontiguousarray(Wo.T),
        "w1": np.ascontiguousarray(W1),
        "w2c": np.ascontiguousarray(W2.reshape(16, 128, 8).transpose(1, 0, 2)),
        "b1c": np.ascontiguousarray(b1.reshape(16, 128).T),
        "b2c": np.ascontiguousarray(b2.reshape(8, 1)),
        "vallT": np.ascontiguousarray(V.transpose(2, 0, 1).reshape(C, RTOT)),
        "uallT": np.ascontiguousarray(U.transpose(0, 2, 1).reshape(RTOT, C)),
    }
    in_maps = []
    for core in range(8):
        b, half = core // 2, core % 2
        xc = np.ascontiguousarray(x[b, half * TOK:(half + 1) * TOK, :].T)
        m = dict(shared)
        m["xT"] = xc
        in_maps.append(m)

    if not _NC_CACHE:
        _NC_CACHE.append(build_program())
    nc = _NC_CACHE[0]
    res = run_bass_kernel_spmd(nc, in_maps, core_ids=list(range(8)))
    out = np.empty((B, T, C), dtype=np.float32)
    for core in range(8):
        b, half = core // 2, core % 2
        out[b, half * TOK:(half + 1) * TOK, :] = res.results[core]["outT"].T
    return out
